# version 19
# baseline (speedup 1.0000x reference)
import sys
if "/opt/trn_rl_repo" not in sys.path:
    sys.path.insert(0, "/opt/trn_rl_repo")

import contextlib
import numpy as np
import ml_dtypes
import concourse.bass as bass
import concourse.mybir as mybir
from concourse.bass_utils import run_bass_kernel_spmd

F32 = mybir.dt.float32
BF16 = mybir.dt.bfloat16
AF = mybir.ActivationFunctionType
ALU = mybir.AluOpType
BB = ml_dtypes.bfloat16

HD = 256
ROT = 64
NQC = 4


class Cfg:
    def __init__(self, T=2048, E=4096, NH=8):
        self.T = T
        self.E = E
        self.NH = NH
        self.ET = E // 128
        self.DH = NH * HD
        self.DT = self.DH // 128
        self.FH = self.DH
        self.FT = self.FH // 128
        self.QCW = T // NQC
        self.KJQ = self.QCW // 128
        self.KJT = T // 128
        self.G = NH // 2
        assert self.FH * 2 == E and self.QCW % 128 == 0


REAL = Cfg()


def _pmaj(w, inner, ET):
    nblk = w.shape[1] // inner
    t = w.reshape(ET, 128, nblk, inner).transpose(2, 1, 0, 3)
    return np.ascontiguousarray(t.reshape(nblk, 128, ET * inner))


def make_inputs(inp, cfg=REAL):
    T, E, NH, ET, FT = cfg.T, cfg.E, cfg.NH, cfg.ET, cfg.FT
    x = np.asarray(inp["L0_attn_hidden_states"])
    Wq = np.asarray(inp["L0_attn_Wq"])
    Wk = np.asarray(inp["L0_attn_Wk"])
    Wv = np.asarray(inp["L0_attn_Wv"])
    Wo = np.asarray(inp["L0_attn_Wo"])
    Wf = np.asarray(inp["L0_ffn_weight"])
    bias = np.asarray(inp["L0_ffn_bias"])
    gw = np.asarray(inp["L0_ffn_gate_w"])
    ics = np.asarray(inp["L0_ffn_in_channel_scale"])
    ocs = np.asarray(inp["L0_ffn_out_channel_scale"])

    inv_freq = 1.0 / 10000.0 ** (np.arange(0, ROT, 2, dtype=np.float64) / ROT)
    sinu = np.outer(np.arange(T, dtype=np.float64), inv_freq)
    ctab = np.ascontiguousarray(np.repeat(np.cos(sinu).T, 2, axis=0)).astype(BB)
    stab = np.ascontiguousarray(np.repeat(np.sin(sinu).T, 2, axis=0)).astype(BB)
    a = np.arange(128)[:, None]
    col = np.arange(cfg.QCW)[None, :]
    masks = np.concatenate(
        [(128 * r + a <= col).astype(BB) for r in range(cfg.KJQ)], axis=1)

    NHT = 2 * NH
    perm = np.concatenate(
        [np.arange(HD * hh, HD * hh + HD)
         for g in range(cfg.G)
         for hh in (2 * g, 2 * g + 1, NHT // 2 + 2 * g, NHT // 2 + 2 * g + 1)])

    gw_p = np.ascontiguousarray(
        gw.T.reshape(ET, 128, 8).transpose(1, 0, 2).reshape(128, ET * 8))

    maps = []
    for c in range(8):
        b, h = c >> 1, c & 1
        fh = slice(cfg.FH * h, cfg.FH * h + cfg.FH)
        xt = np.ascontiguousarray(x[b].T)
        maps.append({
            "xT": xt,
            "xresT": np.ascontiguousarray(xt[fh]),
            "wq": _pmaj(np.ascontiguousarray(Wq[:, fh]), 128, ET),
            "wk": _pmaj(np.ascontiguousarray(Wk[:, fh]), 128, ET),
            "wv": _pmaj(np.ascontiguousarray(Wv[:, fh]), HD, ET),
            "wo": _pmaj(np.ascontiguousarray(Wo[perm][:, fh]), 128, ET),
            "wt": _pmaj(np.ascontiguousarray(Wf[fh, :].T), 128, ET),
            "gw": gw_p,
            "ics": np.ascontiguousarray(ics),
            "ocs": np.ascontiguousarray(ocs[:, fh]),
            "bias_t": np.ascontiguousarray(bias[fh].reshape(FT, 128).T),
            "ctab": ctab, "stab": stab, "masks": masks,
        })
    return maps


def build_program(cfg=REAL, debug=False, race_detect=True):
    T, E, NH = cfg.T, cfg.E, cfg.NH
    ET, DH, DT, FH, FT = cfg.ET, cfg.DH, cfg.DT, cfg.FH, cfg.FT
    QCW, KJQ, KJT, G = cfg.QCW, cfg.KJQ, cfg.KJT, cfg.G

    nc = bass.Bass("TRN2", target_bir_lowering=False,
                   detect_race_conditions=race_detect)

    din = lambda n, s, d=F32: nc.dram_tensor(n, s, d, kind="ExternalInput")
    xT = din("xT", [E, T])
    xresT = din("xresT", [FH, T])
    wq = din("wq", [DT, 128, ET * 128])
    wk = din("wk", [DT, 128, ET * 128])
    wv = din("wv", [NH, 128, ET * HD])
    wo = din("wo", [FT, 128, ET * 128])
    wt = din("wt", [FT, 128, ET * 128])
    gw = din("gw", [128, ET * 8])
    ics = din("ics", [8, E])
    ocs = din("ocs", [8, FH])
    bias_t = din("bias_t", [128, FT])
    ctab = din("ctab", [ROT, T], BF16)
    stab = din("stab", [ROT, T], BF16)
    masks = din("masks", [128, cfg.KJQ * QCW], BF16)

    outT = nc.dram_tensor("outT", [FH, T], F32, kind="ExternalOutput")

    spill_kind = "ExternalOutput" if debug else "Internal"
    q_d = nc.dram_tensor("q_d", [DH, T], BF16, kind=spill_kind)
    k_d = nc.dram_tensor("k_d", [DH, T], BF16, kind=spill_kind)
    v_d = nc.dram_tensor("v_d", [NH, T, HD], BF16, kind=spill_kind)
    g_d = nc.dram_tensor("g_d", [FH, T], BF16, kind=spill_kind)
    xm_tap = (nc.dram_tensor("xm_tap", [128, ET * T], BF16, kind="ExternalOutput")
              if debug else None)
    at_tap = (nc.dram_tensor("at_tap", [DH, T], BF16, kind="ExternalOutput")
              if debug else None)
    cc_in = [nc.dram_tensor(f"cc_in{h}", [HD, T], BF16) for h in range(NH)]
    cc_out = [nc.dram_tensor(f"cc_out{h}", [2 * HD, T], BF16) for h in range(NH)]

    ctx = contextlib.ExitStack()
    sem = lambda n: ctx.enter_context(nc.semaphore(n))

    s_pre = sem("s_pre")
    s_pm = sem("s_pm")
    s_x = [sem(f"s_x{i}") for i in range(4)]
    s_conv = sem("s_conv")
    s_sq = sem("s_sq")
    s_stm = sem("s_stm")
    s_fs = sem("s_fs")
    s_fv = sem("s_fv")
    s_bc = sem("s_bc")
    s_rbd = sem("s_rbd")
    s_nv = sem("s_nv")
    s_ng = sem("s_ng")
    s_rt = sem("s_rt")
    s_wl = [sem(f"s_wl{i}") for i in range(2)]
    s_qkm = sem("s_qkm")
    s_qke = sem("s_qke")
    s_rope = sem("s_rope")
    s_spl = [sem(f"s_spl{i}") for i in range(2)]
    s_wv = [sem(f"s_wv{i}") for i in range(2)]
    s_wvh = sem("s_wvh")
    s_ics = sem("s_ics")
    s_vm = sem("s_vm")
    s_ve = sem("s_ve")
    s_vs = [sem(f"s_vs{i}") for i in range(4)]
    s_is = sem("s_is")
    s_ise = sem("s_ise")
    s_xm = sem("s_xm")
    s_xmv = sem("s_xmv")
    s_ism = sem("s_ism")
    s_icm = sem("s_icm")
    s_sl = [sem(f"s_sl{i}") for i in range(2)]
    s_iseh = sem("s_iseh")
    s_xmg = sem("s_xmg")
    s_xms = sem("s_xms")
    s_wt = [sem(f"s_wt{i}") for i in range(2)]
    s_sgn = sem("s_sgn")
    s_osc = sem("s_osc")
    s_osce = sem("s_osce")
    s_gm = sem("s_gm")
    s_gt = sem("s_gt")
    s_sil = sem("s_sil")
    s_gs = [sem(f"s_gs{i}") for i in range(2)]
    s_qh = [sem(f"s_qh{i}") for i in range(2)]
    s_L = sem("s_L")
    s_exp = sem("s_exp")
    s_msk = sem("s_msk")
    s_att = sem("s_att")
    s_dna = sem("s_dna")
    s_dnr = sem("s_dnr")
    s_dnc = sem("s_dnc")
    s_dnb = sem("s_dnb")
    s_ae = sem("s_ae")
    s_ci = [[sem(f"s_ci{i}_{p}") for p in range(2)] for i in range(NH)]
    s_cc = sem("s_cc")
    s_bl = [sem(f"s_bl{i}") for i in range(G)]
    s_wo = [sem(f"s_wo{i}") for i in range(2)]
    s_sm = sem("s_sm")
    s_gl = [sem(f"s_gl{i}") for i in range(2)]
    s_xr = [sem(f"s_xr{i}") for i in range(2)]
    s_c1 = sem("s_c1")
    s_c2 = sem("s_c2")
    s_st = sem("s_st")
    s_sse = sem("s_sse")
    s_vse = sem("s_vse")
    s_gse = sem("s_gse")

    class _Self:
        def __init__(self, s):
            self.s, self.n = s, 0

        def prod(self, ins):
            ins.then_inc(self.s, 1)
            self.n += 1
            return ins

        def cons(self, eng):
            eng.wait_ge(self.s, self.n)

    se_s, se_v, se_g = _Self(s_sse), _Self(s_vse), _Self(s_gse)

    blk = ctx.enter_context(nc.Block())

    ones_col = ctx.enter_context(nc.sbuf_tensor("ones_col", [128, 1], BF16))
    ones_sc = ctx.enter_context(nc.sbuf_tensor("ones_sc", [128, 1], BF16))
    ones_row = ctx.enter_context(nc.sbuf_tensor("ones_row", [1, 128], BF16))
    ones_dz = ctx.enter_context(nc.sbuf_tensor("ones_dz", [128, 128], BF16))
    ctab_sb = ctx.enter_context(nc.sbuf_tensor("ctab_sb", [ROT, T], BF16))
    stab_sb = ctx.enter_context(nc.sbuf_tensor("stab_sb", [ROT, T], BF16))
    routing = ctx.enter_context(nc.sbuf_tensor("routing", [128, T], BF16))
    gw_sb = ctx.enter_context(nc.sbuf_tensor("gw_sb", [128, ET * 8], BF16))
    ocs_sb = ctx.enter_context(nc.sbuf_tensor("ocs_sb", [128, FH], BF16))
    bias_sb = ctx.enter_context(nc.sbuf_tensor("bias_sb", [128, FT], F32))

    @blk.gpsimd
    def _(g):
        g.memset(ones_col[:, :], 1.0).then_inc(s_pm, 1)
        g.memset(ones_sc[:, :], 1.0 / E).then_inc(s_pm, 1)
        g.memset(ones_row[:, :], 1.0).then_inc(s_pm, 1)
        se_g.prod(g.memset(ones_dz[:, :], 0.0))
        se_g.cons(g)
        g.memset(ones_dz[:, 0:1], 1.0).then_inc(s_pm, 1)
        g.memset(ocs_sb[:, :], 0.0).then_inc(s_pm, 1)
        g.memset(routing[:, :], 0.0).then_inc(s_pm, 1)
        g.dma_start(out=ctab_sb[:, :], in_=ctab[:, :]).then_inc(s_pre, 16)
        g.dma_start(out=stab_sb[:, :], in_=stab[:, :]).then_inc(s_pre, 16)
        g.dma_start(out=gw_sb[:, :], in_=gw[:, :]).then_inc(s_pre, 16)
        g.wait_ge(s_pm, 5)
        g.dma_start(out=ocs_sb[0:8, :], in_=ocs[:, :]).then_inc(s_pre, 16)
        g.dma_start(out=bias_sb[:, :], in_=bias_t[:, :]).then_inc(s_pre, 16)

    PRE = 5 * 16
    PM = 6

    big_ctx = contextlib.ExitStack()
    big = big_ctx.enter_context(nc.sbuf_tensor("big", [128, ET * T], BF16))
    bsl = lambda i: slice(T * i, T * i + T)

    p0l = contextlib.ExitStack()
    r_bf = p0l.enter_context(nc.sbuf_tensor("r_bf", [1, T], BF16))
    mr_bf = p0l.enter_context(nc.sbuf_tensor("mr_bf", [1, T], BF16))
    rb_sb = p0l.enter_context(nc.sbuf_tensor("rb_sb", [128, T], BF16))
    mrb_sb = p0l.enter_context(nc.sbuf_tensor("mrb_sb", [128, T], BF16))
    tn_v = p0l.enter_context(nc.sbuf_tensor("tn_v", [128, T], BF16))
    tn_g = p0l.enter_context(nc.sbuf_tensor("tn_g", [128, T], BF16))
    p0 = contextlib.ExitStack()
    xsq = p0.enter_context(nc.sbuf_tensor("xsq", [128, 2 * T], BF16))
    m_row = p0.enter_context(nc.sbuf_tensor("m_row", [1, T], F32))
    b_row = p0.enter_context(nc.sbuf_tensor("b_row", [1, T], F32))
    msq_bf = p0.enter_context(nc.sbuf_tensor("msq_bf", [1, T], BF16))

    ps0 = contextlib.ExitStack()
    ps_sum = ps0.enter_context(nc.psum_tensor("ps_sum", [1, T], F32))
    ps_sq = ps0.enter_context(nc.psum_tensor("ps_sq", [1, T], F32))

    NCC = T // QCW
    SPB = max(1, 512 // QCW)

    @blk.gpsimd
    def _(g):
        for i in range(ET):
            if i >= 4:
                g.wait_ge(s_x[i % 4], 16 * (i // 4))
            g.dma_start(out=big[:, bsl(i)],
                        in_=xT[128 * i:128 * i + 128, :]).then_inc(s_x[i % 4], 16)

    @blk.vector
    def _(v):
        for i in range(ET):
            sl = slice((i % 2) * T, (i % 2) * T + T)
            v.wait_ge(s_x[i % 4], 16 * (i // 4 + 1))
            if i >= 2:
                v.wait_ge(s_stm, i - 1)
            v.tensor_tensor(xsq[:, sl], big[:, bsl(i)], big[:, bsl(i)],
                            ALU.mult).then_inc(s_sq, 1)

    @blk.tensor
    def _(t):
        t.wait_ge(s_pm, PM)
        for i in range(ET):
            t.wait_ge(s_x[i % 4], 16 * (i // 4 + 1))
            for c in range(NCC):
                t.matmul(ps_sum[0:1, QCW * c:QCW * c + QCW], ones_sc[:, :],
                         big[:, T * i + QCW * c:T * i + QCW * c + QCW],
                         start=(i == 0 and c % SPB == 0), stop=(i == ET - 1),
                         skip_group_check=True)
            t.wait_ge(s_sq, i + 1)
            for c in range(NCC):
                ins = t.matmul(
                    ps_sq[0:1, QCW * c:QCW * c + QCW], ones_sc[:, :],
                    xsq[:, (i % 2) * T + QCW * c:(i % 2) * T + QCW * c + QCW],
                    start=(i == 0 and c % SPB == 0), stop=(i == ET - 1),
                    skip_group_check=True)
                if c == NCC - 1:
                    ins.then_inc(s_stm, 1)

    @blk.scalar
    def _(s):
        s.wait_ge(s_stm, ET)
        se_s.prod(s.copy(m_row[:, :], ps_sum[0:1, :]))
        s.copy(b_row[:, :], ps_sq[0:1, :])
        se_s.cons(s)
        s.activation(msq_bf[:, :], m_row[:, :], AF.Square).then_inc(s_fs, 1)

    @blk.vector
    def _(v):
        v.wait_ge(s_fs, 1)
        se_v.prod(v.tensor_tensor(b_row[:, :], b_row[:, :], msq_bf[:, :],
                                  ALU.subtract))
        se_v.cons(v)
        v.tensor_scalar(b_row[:, :], b_row[:, :], 1e-5, None,
                        ALU.add).then_inc(s_fv, 1)

    @blk.scalar
    def _(s):
        s.wait_ge(s_fv, 1)
        se_s.prod(s.activation(b_row[:, :], b_row[:, :], AF.Ln))
        se_s.cons(s)
        s.activation(r_bf[:, :], b_row[:, :], AF.Exp,
                     scale=-0.5).then_inc(s_fs, 1)

    @blk.vector
    def _(v):
        v.wait_ge(s_fs, 2)
        v.tensor_tensor(mr_bf[:, :], m_row[:, :], r_bf[:, :],
                        ALU.mult).then_inc(s_fv, 2)

    ps0.close()
    p0.close()
    ps0b = contextlib.ExitStack()
    ps_rb = ps0b.enter_context(nc.psum_tensor("ps_rb", [128, T], F32))
    ps_mrb = ps0b.enter_context(nc.psum_tensor("ps_mrb", [128, T], F32))

    @blk.tensor
    def _(t):
        t.wait_ge(s_fv, 3)
        for c in range(NCC):
            t.matmul(ps_rb[:, QCW * c:QCW * c + QCW], ones_row[:, :],
                     r_bf[0:1, QCW * c:QCW * c + QCW], start=True, stop=True)
            ins = t.matmul(ps_mrb[:, QCW * c:QCW * c + QCW], ones_row[:, :],
                           mr_bf[0:1, QCW * c:QCW * c + QCW], start=True, stop=True)
            if c == NCC - 1:
                ins.then_inc(s_bc, 1)

    @blk.scalar
    def _(s):
        s.wait_ge(s_bc, 1)
        s.copy(rb_sb[:, :], ps_rb[:, :]).then_inc(s_rbd, 1)
        s.copy(mrb_sb[:, :], ps_mrb[:, :]).then_inc(s_rbd, 1)

    NGT = max(round(ET * 0.34), 1)
    gp_tiles = list(range(ET - NGT, ET))
    vec_tiles = [i for i in range(ET) if i not in gp_tiles]

    p1r = contextlib.ExitStack()
    rl_exp = p1r.enter_context(nc.sbuf_tensor("rl_exp", [8, T], BF16))
    rs_bf = p1r.enter_context(nc.sbuf_tensor("rs_bf", [1, T], BF16))
    p1qk = contextlib.ExitStack()
    WB = ET * 128
    wring = p1qk.enter_context(nc.sbuf_tensor("wring", [128, 2 * WB], BF16))
    qbf = p1qk.enter_context(nc.sbuf_tensor("qbf", [128, 2 * T], BF16))
    qs = p1qk.enter_context(nc.sbuf_tensor("qs", [ROT, T], BF16))

    norm_seq = []
    iv = ig = 0
    while iv < len(vec_tiles) or ig < len(gp_tiles):
        tv = 5.2 * (iv + 1) if iv < len(vec_tiles) else 1e9
        tg = 9.6 * (ig + 1) if ig < len(gp_tiles) else 1e9
        if tv <= tg:
            norm_seq.append((vec_tiles[iv], "v", iv + 1))
            iv += 1
        else:
            norm_seq.append((gp_tiles[ig], "g", ig + 1))
            ig += 1

    @blk.gpsimd
    def _(g):
        g.wait_ge(s_stm, ET)
        g.wait_ge(s_fv, 3)
        for j in range(2):
            g.dma_start(out=wring[:, (j % 2) * WB:(j % 2) * WB + WB],
                        in_=wq[j, :, :]).then_inc(s_wl[j % 2], 16)

    @blk.vector
    def _(v):
        v.wait_ge(s_rbd, 2)
        for n, i in enumerate(vec_tiles):
            if n >= 1:
                v.wait_ge(s_nv, n)
            se_v.prod(v.tensor_tensor(tn_v[:, :], big[:, bsl(i)], rb_sb[:, :],
                                      ALU.mult))
            se_v.cons(v)
            v.tensor_tensor(big[:, bsl(i)], tn_v[:, :], mrb_sb[:, :],
                            ALU.subtract).then_inc(s_nv, 1)

    @blk.gpsimd
    def _(g):
        g.wait_ge(s_rbd, 2)
        for n, i in enumerate(gp_tiles):
            if n >= 1:
                g.wait_ge(s_ng, n)
            se_g.prod(g.tensor_tensor(tn_g[:, :], big[:, bsl(i)], rb_sb[:, :],
                                      ALU.mult))
            se_g.cons(g)
            g.tensor_tensor(big[:, bsl(i)], tn_g[:, :], mrb_sb[:, :],
                            ALU.subtract).then_inc(s_ng, 1)

    ps0b.close()

    ps1 = contextlib.ExitStack()
    psq = [ps1.enter_context(nc.psum_tensor(f"psq{j}", [128, T], F32))
           for j in range(2)]
    ps_rl = psq[1][0:8, :]
    ps_rs = psq[1][0:1, :]
    ps_rbc = psq[1][0:8, :]

    @blk.tensor
    def _(t):
        t.wait_ge(s_pre, PRE)
        first_e = norm_seq[0][0]
        last_e = norm_seq[-1][0]
        for i, which, cnt in norm_seq:
            if which == "v":
                t.wait_ge(s_nv, cnt)
            else:
                t.wait_ge(s_ng, cnt)
            for c in range(NCC):
                ins = t.matmul(ps_rl[:, QCW * c:QCW * c + QCW],
                               gw_sb[:, 8 * i:8 * i + 8],
                               big[:, T * i + QCW * c:T * i + QCW * c + QCW],
                               start=(i == first_e and c % SPB == 0),
                               stop=(i == last_e), skip_group_check=True)
                if i == last_e and c == NCC - 1:
                    ins.then_inc(s_rt, 1)
            if i == first_e:
                t.wait_ge(s_wl[0], 16)
            for c in range(NCC):
                ins = t.matmul(psq[0][:, QCW * c:QCW * c + QCW],
                               wring[:, 128 * i:128 * i + 128],
                               big[:, T * i + QCW * c:T * i + QCW * c + QCW],
                               start=(i == first_e and c % SPB == 0),
                               stop=(i == last_e), skip_group_check=True)
                if i == last_e and c == NCC - 1:
                    ins.then_inc(s_qkm, 1)

    @blk.scalar
    def _(s):
        s.wait_ge(s_rt, 1)
        s.activation(rl_exp[:, :], ps_rl[:, :], AF.Exp).then_inc(s_rt, 1)

    @blk.tensor
    def _(t):
        t.wait_ge(s_rt, 2)
        for c in range(NCC):
            ins = t.matmul(ps_rs[:, QCW * c:QCW * c + QCW], ones_col[0:8, :],
                           rl_exp[:, QCW * c:QCW * c + QCW], start=True, stop=True)
            if c == NCC - 1:
                ins.then_inc(s_rt, 1)

    @blk.scalar
    def _(s):
        s.wait_ge(s_rt, 3)
        se_s.prod(s.activation(ps_rs[:, :], ps_rs[:, :], AF.Ln))
        se_s.cons(s)
        s.activation(rs_bf[:, :], ps_rs[:, :], AF.Exp,
                     scale=-1.0).then_inc(s_rt, 2)

    @blk.tensor
    def _(t):
        t.wait_ge(s_rt, 5)
        for c in range(NCC):
            ins = t.matmul(ps_rbc[:, QCW * c:QCW * c + QCW], ones_row[0:1, 0:8],
                           rs_bf[0:1, QCW * c:QCW * c + QCW], start=True, stop=True)
            if c == NCC - 1:
                ins.then_inc(s_rt, 1)

    @blk.vector
    def _(v):
        v.wait_ge(s_rt, 6)
        v.tensor_tensor(routing[0:8, :], rl_exp[:, :], ps_rbc[:, :],
                        ALU.mult).then_inc(s_rt, 1)

    SWAP = []
    for i in range(16):
        SWAP += [2 * i + 1, 2 * i]

    @blk.gpsimd
    def _(g):
        for j in range(2, 2 * DT):
            g.wait_ge(s_qkm, j - 1)
            src = wq if j < DT else wk
            g.dma_start(out=wring[:, (j % 2) * WB:(j % 2) * WB + WB],
                        in_=src[j % DT, :, :]).then_inc(s_wl[j % 2], 16)

    @blk.tensor
    def _(t):
        for j in range(1, 2 * DT):
            if j == 1:
                t.wait_ge(s_rt, 7)
            t.wait_ge(s_wl[j % 2], 16 * (j // 2 + 1))
            if j >= 2:
                t.wait_ge(s_qke, j - 1)
            wof = (j % 2) * WB
            for e in range(ET):
                for c in range(NCC):
                    ins = t.matmul(
                        psq[j % 2][:, QCW * c:QCW * c + QCW],
                        wring[:, wof + 128 * e:wof + 128 * e + 128],
                        big[:, T * e + QCW * c:T * e + QCW * c + QCW],
                        start=(e == 0 and c % SPB == 0), stop=(e == ET - 1),
                        skip_group_check=True)
                    if e == ET - 1 and c == NCC - 1:
                        ins.then_inc(s_qkm, 1)


    @blk.scalar
    def _(s):
        for j in range(2 * DT):
            s.wait_ge(s_qkm, j + 1)
            if j >= 2:
                s.wait_ge(s_spl[j % 2], 16 * (j // 2))
            s.activation(qbf[:, (j % 2) * T:(j % 2) * T + T], psq[j % 2][:, :],
                         AF.Copy).then_inc(s_qke, 1)

    @blk.vector
    def _(v):
        v.wait_ge(s_pre, PRE)
        nrope_v = 0
        for j in range(2 * DT):
            if (j % DT) % 2 != 0:
                continue
            sl = slice((j % 2) * T, (j % 2) * T + T)
            v.wait_ge(s_qke, j + 1)
            if nrope_v >= 1:
                v.wait_ge(s_rope, nrope_v)
            nrope_v += 1
            se_v.prod(v.stream_shuffle(qs[:, :], qbf[0:ROT, sl], SWAP))
            se_v.cons(v)
            se_v.prod(v.tensor_tensor(qbf[0:ROT, sl], qbf[0:ROT, sl],
                                      ctab_sb[:, :], ALU.mult))
            se_v.cons(v)
            se_v.prod(v.tensor_tensor(qs[:, :], qs[:, :], stab_sb[:, :],
                                      ALU.mult))
            se_v.cons(v)
            v.tensor_tensor(qbf[0:ROT, sl], qbf[0:ROT, sl], qs[:, :],
                            ALU.add).then_inc(s_rope, 1)

    @blk.sync
    def _(sy):
        nrope = 0
        for j in range(2 * DT):
            dt = j % DT
            if dt % 2 == 0:
                nrope += 1
                sy.wait_ge(s_rope, nrope)
            else:
                sy.wait_ge(s_qke, j + 1)
            dst = q_d if j < DT else k_d
            sy.dma_start(out=dst[128 * dt:128 * dt + 128, :],
                         in_=qbf[:, (j % 2) * T:(j % 2) * T + T]
                         ).then_inc(s_spl[j % 2], 16)

    ps1.close()
    p1qk.close()
    p1r.close()
    p0l.close()

    p1bv = contextlib.ExitStack()
    ics_sb = p1bv.enter_context(nc.sbuf_tensor("ics_sb", [128, E], BF16))
    p1v = contextlib.ExitStack()
    VB = ET * HD
    wvb = p1v.enter_context(nc.sbuf_tensor("wvb", [128, 2 * VB], BF16))
    vev = p1v.enter_context(nc.sbuf_tensor("vev", [128, 4 * HD], BF16))
    ps1v = contextlib.ExitStack()
    ps_big = ps1v.enter_context(nc.psum_tensor("ps_big", [128, 2 * T], F32))
    BW = 512
    psv = [ps_big[:, 0:HD], ps_big[:, BW:BW + HD]]
    ps_ish = [ps_big[:, T + (T // 2) * k:T + (T // 2) * (k + 1)]
              for k in range(2)]
    psg = [ps_big[:, T * j:T * j + T] for j in range(2)]

    @blk.gpsimd
    def _(g):
        g.wait_ge(s_qkm, 2 * DT)
        g.dma_start(out=wvb[:, 0:VB // 2],
                    in_=wv[0, :, 0:VB // 2]).then_inc(s_wv[0], 16)
        g.wait_ge(s_spl[0], 16 * DT)
        g.wait_ge(s_spl[1], 16 * DT)
        g.dma_start(out=wvb[:, VB // 2:VB],
                    in_=wv[0, :, VB // 2:VB]).then_inc(s_wvh, 16)
        for h in range(1, NH):
            if h >= 2:
                g.wait_ge(s_vm, KJT * (h - 1))
            g.dma_start(out=wvb[:, (h % 2) * VB:(h % 2) * VB + VB],
                        in_=wv[h, :, :]).then_inc(s_wv[h % 2], 16)
        g.memset(ics_sb[:, :], 0.0).then_inc(s_icm, 1)
        g.wait_ge(s_icm, 1)
        g.dma_start(out=ics_sb[0:8, :], in_=ics[:, :]).then_inc(s_ics, 16)

    @blk.tensor
    def _(t):
        for h in range(NH):
            if h == 0:
                t.wait_ge(s_qke, 2 * DT)
            t.wait_ge(s_wv[h % 2], 16 * (h // 2 + 1))
            wof = (h % 2) * VB
            for ti in range(KJT):
                gi = KJT * h + ti
                if gi >= 2:
                    t.wait_ge(s_ve, gi - 1)
                for e in range(ET):
                    if h == 0 and ti == 0 and e == ET // 2:
                        t.wait_ge(s_wvh, 16)
                    ins = t.matmul(
                        psv[gi % 2][:, :],
                        big[:, T * e + 128 * ti:T * e + 128 * ti + 128],
                        wvb[:, wof + HD * e:wof + HD * e + HD],
                        start=(e == 0), stop=(e == ET - 1), skip_group_check=True)
                    if e == ET - 1:
                        ins.then_inc(s_vm, 1)

    @blk.scalar
    def _(s):
        for gi in range(KJT * NH):
            s.wait_ge(s_vm, gi + 1)
            if gi >= 4:
                s.wait_ge(s_vs[gi % 4], 16 * (gi // 4))
            s.activation(vev[:, (gi % 4) * HD:(gi % 4) * HD + HD],
                         psv[gi % 2][:, :], AF.Copy).then_inc(s_ve, 1)
            if gi % 2 == 1:
                h, ti = gi // KJT, gi % KJT
                s.wait_ge(s_ve, gi + 1)
                s.dma_start(
                    out=v_d[h, 128 * ti:128 * ti + 128, :],
                    in_=vev[:, (gi % 4) * HD:(gi % 4) * HD + HD]
                ).then_inc(s_vs[gi % 4], 16)

    @blk.sync
    def _(sy):
        for h in range(NH):
            for ti in range(KJT):
                gi = KJT * h + ti
                if gi % 2 == 1:
                    continue
                sy.wait_ge(s_ve, gi + 1)
                sy.dma_start(
                    out=v_d[h, 128 * ti:128 * ti + 128, :],
                    in_=vev[:, (gi % 4) * HD:(gi % 4) * HD + HD]
                ).then_inc(s_vs[gi % 4], 16)

    p1v.close()

    NGX = max(round(ET * 0.34), 1)
    NVX = ET - NGX

    ISQ = []
    _iv = _ig = 0
    while _iv < NVX or _ig < NGX:
        _tv = 2.6 * (_iv + 1) if _iv < NVX else 1e9
        _tg = 4.8 * (_ig + 1) if _ig < NGX else 1e9
        if _tv <= _tg:
            ISQ.append(_iv)
            _iv += 1
        else:
            ISQ.append(NVX + _ig)
            _ig += 1
    NR = 2 * ET
    pos_of = {t_: p_ for p_, t_ in enumerate(ISQ)}
    evac_cnt = []
    vec_sl_cnt = []
    _n = 0
    _v = [0, 0]
    for r_ in range(NR):
        if ISQ[r_ // 2] >= NVX:
            _n += 1
        else:
            _v[r_ % 2] += 1
        evac_cnt.append(_n)
        vec_sl_cnt.append(_v[r_ % 2])

    def slot_wait(eng, pr):
        if ISQ[pr // 2] < NVX:
            eng.wait_ge(s_sl[pr % 2], vec_sl_cnt[pr])
        else:
            eng.wait_ge(s_iseh, evac_cnt[pr])

    def xm_wait(eng, i):
        if i < NVX:
            p_ = pos_of[i]
            eng.wait_ge(s_sl[0], vec_sl_cnt[2 * p_])
            eng.wait_ge(s_sl[1], vec_sl_cnt[2 * p_ + 1])
        else:
            eng.wait_ge(s_xmg, i - NVX + 1)

    p2 = contextlib.ExitStack()
    wtr = p2.enter_context(nc.sbuf_tensor("wtr", [128, 2 * WB], BF16))
    is_sb = p2.enter_context(nc.sbuf_tensor("is_sb", [128, 2 * T], BF16))
    oscale = p2.enter_context(nc.sbuf_tensor("oscale", [128, 2 * T], BF16))
    tsb = p2.enter_context(nc.sbuf_tensor("tsb", [128, 2 * T], BF16))
    gbf = p2.enter_context(nc.sbuf_tensor("gbf", [128, 2 * T], BF16))

    @blk.gpsimd
    def _(g):
        g.wait_ge(s_vm, KJT * (NH - 1))
        g.dma_start(out=wtr[:, 0:WB], in_=wt[0, :, :]).then_inc(s_wt[0], 16)
        for _k in range(4):
            g.wait_ge(s_vs[_k], 16 * (KJT * NH // 4))
        g.dma_start(out=wtr[:, WB:2 * WB], in_=wt[1, :, :]).then_inc(s_wt[1], 16)
        gp_idx = 0
        for p, i in enumerate(ISQ):
            if i < NVX:
                continue
            g.wait_ge(s_iseh, evac_cnt[2 * p + 1])
            g.tensor_tensor(big[:, bsl(i)], big[:, bsl(i)],
                            is_sb[:, (gp_idx % 2) * T:(gp_idx % 2) * T + T],
                            ALU.mult).then_inc(s_xmg, 1)
            gp_idx += 1
        for ot in range(2, FT):
            g.wait_ge(s_gm, ot - 1)
            g.dma_start(out=wtr[:, (ot % 2) * WB:(ot % 2) * WB + WB],
                        in_=wt[ot, :, :]).then_inc(s_wt[ot % 2], 16)

    @blk.vector
    def _(v):
        for p, i in enumerate(ISQ):
            if i >= NVX:
                continue
            for half in range(2):
                r = 2 * p + half
                hsl = slice(T * i + half * (T // 2),
                            T * i + half * (T // 2) + T // 2)
                v.wait_ge(s_ism, r + 1)
                v.tensor_tensor(big[:, hsl], big[:, hsl],
                                ps_ish[r % 2][:, :],
                                ALU.mult).then_inc(s_sl[r % 2], 1)

    if debug:
        @blk.sync
        def _(sy):
            for i in range(ET):
                xm_wait(sy, i)
                sy.dma_start(out=xm_tap[:, bsl(i)],
                             in_=big[:, bsl(i)]).then_inc(s_xms, 16)

    @blk.scalar
    def _(s):
        s.wait_ge(s_pre, PRE)
        s.wait_ge(s_wt[0], 16)
        s.sign(wtr[:, 0:WB], wtr[:, 0:WB]).then_inc(s_sgn, 1)
        s.wait_ge(s_osc, 1)
        s.copy(oscale[:, 0:T], psg[0][:, :]).then_inc(s_osce, 1)
        gp_idx = 0
        for r in range(NR):
            p = r // 2
            if ISQ[p] < NVX:
                continue
            s.wait_ge(s_ism, r + 1)
            if gp_idx >= 4 and r % 2 == 0:
                s.wait_ge(s_xmg, gp_idx // 2 - 1)
            s.activation(is_sb[:, (gp_idx // 2 % 2) * T + (r % 2) * (T // 2):
                                 (gp_idx // 2 % 2) * T + (r % 2) * (T // 2)
                                 + T // 2],
                         ps_ish[r % 2][:, :], AF.Copy).then_inc(s_iseh, 1)
            gp_idx += 1
        for ot in range(1, FT):
            s.wait_ge(s_wt[ot % 2], 16 * (ot // 2 + 1))
            if ot >= 2:
                s.wait_ge(s_gm, ot - 1)
            sl = slice((ot % 2) * WB, (ot % 2) * WB + WB)
            s.sign(wtr[:, sl], wtr[:, sl]).then_inc(s_sgn, 1)
            s.wait_ge(s_osc, ot + 1)
            s.copy(oscale[:, (ot % 2) * T:(ot % 2) * T + T],
                   psg[ot % 2][:, :]).then_inc(s_osce, 1)
            o = ot - 1
            s.wait_ge(s_gt, o + 1)
            if o >= 2:
                s.wait_ge(s_gs[o % 2], 16 * (o // 2))
            s.activation(gbf[:, (o % 2) * T:(o % 2) * T + T],
                         tsb[:, (o % 2) * T:(o % 2) * T + T], AF.Silu,
                         bias=bias_sb[:, o:o + 1], scale=1.0).then_inc(s_sil, 1)
        o = FT - 1
        s.wait_ge(s_gt, o + 1)
        s.wait_ge(s_gs[o % 2], 16 * (o // 2))
        s.activation(gbf[:, (o % 2) * T:(o % 2) * T + T],
                     tsb[:, (o % 2) * T:(o % 2) * T + T], AF.Silu,
                     bias=bias_sb[:, o:o + 1], scale=1.0).then_inc(s_sil, 1)

    @blk.tensor
    def _(t):
        t.wait_ge(s_ics, 16)
        t.wait_ge(s_icm, 1)
        t.wait_ge(s_ve, KJT * NH)
        for _k in range(4):
            t.wait_ge(s_vs[_k], 16 * (KJT * NH // 4))
        t.wait_ge(s_sgn, 1)
        for c in range(NCC):
            ins = t.matmul(psg[0][:, QCW * c:QCW * c + QCW],
                           ocs_sb[:, 0:128],
                           routing[:, QCW * c:QCW * c + QCW],
                           start=True, stop=True, skip_group_check=True)
            if c == NCC - 1:
                ins.then_inc(s_osc, 1)
        t.wait_ge(s_osce, 1)

        def g_mms(t, idx):
            i = ISQ[idx]
            xm_wait(t, i)
            for c in range(NCC):
                ins = t.matmul(
                    psg[0][:, QCW * c:QCW * c + QCW],
                    wtr[:, 128 * i:128 * i + 128],
                    big[:, T * i + QCW * c:T * i + QCW * c + QCW],
                    start=(idx == 0 and c % SPB == 0), stop=(idx == ET - 1),
                    skip_group_check=True)
                if idx == ET - 1 and c == NCC - 1:
                    ins.then_inc(s_gm, 1)

        for r in range(NR):
            i = ISQ[r // 2]
            half = r % 2
            if r >= 2:
                slot_wait(t, r - 2)
            for c in range(NCC // 2):
                cc = half * (NCC // 2) + c
                ins = t.matmul(ps_ish[r % 2][:, QCW * c:QCW * c + QCW],
                               ics_sb[:, 128 * i:128 * i + 128],
                               routing[:, QCW * cc:QCW * cc + QCW],
                               start=True, stop=True, skip_group_check=True)
                if c == NCC // 2 - 1:
                    ins.then_inc(s_ism, 1)
            if r % 2 == 1 and r // 2 >= 2:
                g_mms(t, r // 2 - 2)
        g_mms(t, ET - 2)
        g_mms(t, ET - 1)
        def osc_mms(t, ot):
            for c in range(NCC):
                ins = t.matmul(psg[ot % 2][:, QCW * c:QCW * c + QCW],
                               ocs_sb[:, 128 * ot:128 * ot + 128],
                               routing[:, QCW * c:QCW * c + QCW],
                               start=True, stop=True, skip_group_check=True)
                if c == NCC - 1:
                    ins.then_inc(s_osc, 1)

        for ot in range(1, FT):
            sl = ot % 2
            t.wait_ge(s_sgn, ot + 1)
            if ot == 1:
                slot_wait(t, NR - 2)
                slot_wait(t, NR - 1)
                osc_mms(t, 1)
            t.wait_ge(s_osce, ot + 1)
            wof = sl * WB
            for ht in range(ET):
                if ht == 6 and ot + 1 < FT:
                    t.wait_ge(s_gt, ot)
                    osc_mms(t, ot + 1)
                for c in range(NCC):
                    ins = t.matmul(
                        psg[sl][:, QCW * c:QCW * c + QCW],
                        wtr[:, wof + 128 * ht:wof + 128 * ht + 128],
                        big[:, T * ht + QCW * c:T * ht + QCW * c + QCW],
                        start=(ht == 0 and c % SPB == 0), stop=(ht == ET - 1),
                        skip_group_check=True)
                    if ht == ET - 1 and c == NCC - 1:
                        ins.then_inc(s_gm, 1)

    @blk.vector
    def _(v):
        for ot in range(FT):
            sl = slice((ot % 2) * T, (ot % 2) * T + T)
            v.wait_ge(s_gm, ot + 1)
            v.tensor_tensor(tsb[:, sl], psg[ot % 2][:, :], oscale[:, sl],
                            ALU.mult).then_inc(s_gt, 1)

    @blk.sync
    def _(sy):
        for ot in range(FT):
            sy.wait_ge(s_sil, ot + 1)
            sy.dma_start(out=g_d[128 * ot:128 * ot + 128, :],
                         in_=gbf[:, (ot % 2) * T:(ot % 2) * T + T]
                         ).then_inc(s_gs[ot % 2], 16)

    ps1v.close()
    p2.close()
    p1bv.close()
    big_ctx.close()

    BG2A = min(8, max(ET // 4, 1))
    p4pre = contextlib.ExitStack()
    wor = p4pre.enter_context(nc.sbuf_tensor("wor", [128, 2 * WB], BF16))
    gtr = p4pre.enter_context(nc.sbuf_tensor("gtr", [128, 2 * T], BF16))
    xrr = p4pre.enter_context(nc.sbuf_tensor("xrr", [128, 2 * T], F32))
    two = p4pre.enter_context(nc.sbuf_tensor("two", [128, 2 * T], F32))
    big2a = p4pre.enter_context(nc.sbuf_tensor("big2a", [128, BG2A * T], BF16))
    RG = 4 * HD // 128
    EARLY_G = [gi for gi in range(G) if RG * (gi + 1) <= BG2A]

    p3 = contextlib.ExitStack()
    qh = p3.enter_context(nc.sbuf_tensor("qh", [128, 2 * 2 * T], BF16))
    masks_sb = p3.enter_context(nc.sbuf_tensor("masks_sb", [128, KJQ * QCW], BF16))
    kh = p3.enter_context(nc.sbuf_tensor("kh", [128, 2 * 2 * T], BF16))
    vh = p3.enter_context(nc.sbuf_tensor("vh", [128, 2 * KJT * HD], BF16))
    pT = p3.enter_context(nc.sbuf_tensor("pT", [128, 2 * KJT * QCW], BF16))
    den_f = p3.enter_context(nc.sbuf_tensor("den_f", [1, QCW], F32))
    den_bf = p3.enter_context(nc.sbuf_tensor("den_bf", [1, QCW], BF16))
    rb_at = p3.enter_context(nc.sbuf_tensor("rb_at", [128, QCW], BF16))
    att_ev = p3.enter_context(nc.sbuf_tensor("att_ev", [128, 4 * QCW], BF16))
    ps3 = contextlib.ExitStack()
    psL = [ps3.enter_context(nc.psum_tensor(f"psL{j}", [128, QCW], F32))
           for j in range(4)]
    psO = [ps3.enter_context(nc.psum_tensor(f"psO{j}", [128, QCW], F32))
           for j in range(2)]
    ps_den = ps3.enter_context(nc.psum_tensor("ps_den", [128, QCW], F32))
    ps_drb = ps3.enter_context(nc.psum_tensor("ps_drb", [128, QCW], F32))

    chunks = [(h, qc) for h in range(NH) for qc in range(NQC)]
    nkj = lambda qc: KJQ * (qc + 1)
    cum_L = [0]
    for (_, qc) in chunks:
        cum_L.append(cum_L[-1] + nkj(qc))
    NCH = len(chunks)
    EXPI = []
    blk2ei = {}
    for ci, (_h, _qc) in enumerate(chunks):
        for kj in range(nkj(_qc)):
            blk2ei[cum_L[ci] + kj] = len(EXPI) + 1
            EXPI.append((ci, kj, 1))
    ci_ord = {}
    _cnt = {}
    for ci, (_h, _qc) in enumerate(chunks):
        key = (_h, ci % 2)
        _cnt[key] = _cnt.get(key, 0) + 1
        ci_ord[ci] = _cnt[key]
    CPP = {k: v for k, v in _cnt.items()}

    def cc_rows(gi):
        out = []
        for r in range(RG):
            h = 2 * gi + ((r // 2) % 2)
            out.append((r, h, 128 * (r % 2) + HD * (r // 4)))
        return out
    ei_chunk_done = [0] * (NCH + 1)
    blk_off = {}
    for idx, (ci, kj0, w) in enumerate(EXPI):
        ei_chunk_done[ci + 1] = idx + 1
        for d in range(w):
            blk_off[cum_L[ci] + kj0 + d] = d
    DBF = 2 if debug else 1
    CI_TOT = 16 * 2 * NCH * DBF
    PTW = KJT * QCW
    VHW = KJT * HD

    def emit_head_load(sy, h):
        hb = h % 2
        sy.dma_start(out=qh[:, hb * 2 * T:hb * 2 * T + T],
                     in_=q_d[HD * h:HD * h + 128, :]).then_inc(s_qh[hb], 16)
        sy.dma_start(out=qh[:, hb * 2 * T + T:hb * 2 * T + 2 * T],
                     in_=q_d[HD * h + 128:HD * h + HD, :]).then_inc(s_qh[hb], 16)
        sy.dma_start(out=kh[:, hb * 2 * T:hb * 2 * T + T],
                     in_=k_d[HD * h:HD * h + 128, :]).then_inc(s_qh[hb], 16)
        sy.dma_start(out=kh[:, hb * 2 * T + T:hb * 2 * T + 2 * T],
                     in_=k_d[HD * h + 128:HD * h + HD, :]).then_inc(s_qh[hb], 16)
        sy.dma_start(
            out=vh[:, hb * VHW:hb * VHW + VHW],
            in_=v_d[h].rearrange("(t p) d -> p t d", p=128)).then_inc(s_qh[hb], 16)

    def emit_den(t, ci):
        h, qc = chunks[ci]
        cb = ci % 2
        n = nkj(qc)
        t.wait_ge(s_exp, ei_chunk_done[ci + 1])
        t.wait_ge(s_msk, KJQ * (ci + 1))
        if ci >= 1:
            t.wait_ge(s_dnr, ci)
            t.wait_ge(s_ae, 2 * ci)
        hb = h % 2
        for kj in range(n):
            ins = t.matmul(ps_den[:, :], ones_dz[:, :],
                           pT[:, cb * PTW + QCW * kj:cb * PTW + QCW * kj + QCW],
                           start=(kj == 0), stop=(kj == n - 1),
                           skip_group_check=True)
            if kj == n - 1:
                ins.then_inc(s_dna, 1)
        for dt in range(2):
            for kj in range(n):
                t.matmul(psO[dt][:, :],
                         vh[:, hb * VHW + HD * kj + 128 * dt:
                            hb * VHW + HD * kj + 128 * dt + 128],
                         pT[:, cb * PTW + QCW * kj:cb * PTW + QCW * kj + QCW],
                         start=(kj == 0), stop=(kj == n - 1),
                         skip_group_check=True)

    def emit_av2(t, ci):
        t.wait_ge(s_dnr, ci + 1)
        if ci >= 1:
            t.wait_ge(s_dnb, ci)
        t.matmul(ps_drb[:, :], ones_row[:, :], den_bf[0:1, :],
                 start=True, stop=True).then_inc(s_att, 1)

    @blk.tensor
    def _(t):
        for ci, (h, qc) in enumerate(chunks):
            hb = h % 2
            if qc == 0:
                t.wait_ge(s_qh[h % 2], 16 * 5 * (h // 2 + 1))
            for kj in range(nkj(qc)):
                li = cum_L[ci] + kj
                ei = blk2ei[li]
                if ei > 4:
                    t.wait_ge(s_exp, ei - 4)
                for dt in range(2):
                    ins = t.matmul(
                        psL[(ei - 1) % 4][:, 0:QCW],
                        kh[:, hb * 2 * T + dt * T + 128 * kj:
                           hb * 2 * T + dt * T + 128 * kj + 128],
                        qh[:, hb * 2 * T + dt * T + QCW * qc:
                           hb * 2 * T + dt * T + QCW * qc + QCW],
                        start=(dt == 0), stop=(dt == 1), skip_group_check=True)
                    if dt == 1:
                        ins.then_inc(s_L, 1)
            if ci > 0:
                emit_den(t, ci - 1)
                emit_av2(t, ci - 1)
        emit_den(t, NCH - 1)
        emit_av2(t, NCH - 1)

    @blk.scalar
    def _(s):
        s.wait_ge(s_gs[0], 16 * ((FT + 1) // 2))
        s.wait_ge(s_gs[1], 16 * (FT // 2))
        if debug:
            s.wait_ge(s_xms, 16 * ET)
        for ft in range(2):
            s.dma_start(out=gtr[:, (ft % 2) * T:(ft % 2) * T + T],
                        in_=g_d[128 * ft:128 * ft + 128, :]).then_inc(s_gl[ft % 2], 16)

    def emit_recip(s, ci):
        s.wait_ge(s_dna, ci + 1)
        s.wait_ge(s_att, ci - 1 if ci >= 1 else 0)
        se_s.prod(s.activation(den_f[:, :], ps_den[0:1, :], AF.Ln))
        se_s.cons(s)
        s.activation(den_bf[:, :], den_f[:, :], AF.Exp,
                     scale=-1.0).then_inc(s_dnr, 1)

    @blk.scalar
    def _(s):
        for idx, (ci, kj0, w) in enumerate(EXPI):
            h, qc = chunks[ci]
            cb = ci % 2
            li = cum_L[ci] + kj0
            if kj0 == 0 and ci >= 2:
                s.wait_ge(s_att, ci - 1)
            s.wait_ge(s_L, li + w)
            s.activation(pT[:, cb * PTW + QCW * kj0:
                            cb * PTW + QCW * (kj0 + w)],
                         psL[idx % 4][:, 0:QCW * w], AF.Exp,
                         scale=0.0625).then_inc(s_exp, 1)
            if kj0 + w == nkj(qc) and ci >= 1:
                emit_recip(s, ci - 1)
        emit_recip(s, NCH - 1)

    @blk.vector
    def _(v):
        v.wait_ge(s_pre, PRE + 16)
        for ci, (h, qc) in enumerate(chunks):
            cb = ci % 2
            for r in range(KJQ):
                kj = KJQ * qc + r
                li = cum_L[ci] + kj
                v.wait_ge(s_exp, blk2ei[li])
                sl = slice(cb * PTW + QCW * kj, cb * PTW + QCW * kj + QCW)
                v.tensor_tensor(pT[:, sl], pT[:, sl],
                                masks_sb[:, QCW * r:QCW * r + QCW],
                                ALU.mult).then_inc(s_msk, 1)
            v.wait_ge(s_att, ci + 1)
            if ci >= 1:
                v.wait_ge(s_ae, 2 * ci)
            v.tensor_copy(rb_at[:, :], ps_drb[:, :]).then_inc(s_dnb, 1)
            if ci >= 2:
                hp, _ = chunks[ci - 2]
                v.wait_ge(s_ci[hp][ci % 2],
                          16 * 2 * DBF * ci_ord[ci - 2])
            v.wait_ge(s_dnb, ci + 1)
            ab = (ci % 2) * 2 * QCW
            for dt in range(2):
                v.tensor_tensor(att_ev[:, ab + QCW * dt:ab + QCW * dt + QCW],
                                psO[dt][:, :], rb_at[:, :],
                                ALU.mult).then_inc(s_ae, 1)

    @blk.sync
    def _(sy):
        sy.wait_ge(s_gs[0], 16 * ((FT + 1) // 2))
        sy.wait_ge(s_gs[1], 16 * (FT // 2))
        if debug:
            sy.wait_ge(s_xms, 16 * ET)
        for ft in range(2):
            sy.dma_start(out=xrr[:, (ft % 2) * T:(ft % 2) * T + T],
                         in_=xresT[128 * ft:128 * ft + 128, :]
                         ).then_inc(s_xr[ft % 2], 16)
        emit_head_load(sy, 0)
        emit_head_load(sy, 1)
        for ci, (h, qc) in enumerate(chunks):
            sy.wait_ge(s_ae, 2 * (ci + 1))
            ab = (ci % 2) * 2 * QCW
            for dt in range(2):
                sy.dma_start(
                    out=cc_in[h][128 * dt:128 * dt + 128,
                                 QCW * qc:QCW * qc + QCW],
                    in_=att_ev[:, ab + QCW * dt:ab + QCW * dt + QCW]
                ).then_inc(s_ci[h][ci % 2], 16)
            if debug:
                for dt in range(2):
                    sy.dma_start(
                        out=at_tap[HD * h + 128 * dt:HD * h + 128 * dt + 128,
                                   QCW * qc:QCW * qc + QCW],
                        in_=att_ev[:, ab + QCW * dt:ab + QCW * dt + QCW]
                    ).then_inc(s_ci[h][ci % 2], 16)
            if qc == NQC - 1 and h + 2 < NH:
                sy.wait_ge(s_att, NQC * (h + 1))
                emit_head_load(sy, h + 2)

    @blk.gpsimd
    def _(g):
        g.wait_ge(s_gs[0], 16 * ((FT + 1) // 2))
        g.wait_ge(s_gs[1], 16 * (FT // 2))
        g.dma_start(out=masks_sb[:, :], in_=masks[:, :]).then_inc(s_pre, 16)
        for h in range(NH):
            for p in range(2):
                g.wait_ge(s_ci[h][p], 16 * 2 * DBF * CPP[(h, p)])
            g.collective_compute(
                "AllGather", ALU.bypass,
                replica_groups=[[0, 1], [2, 3], [4, 5], [6, 7]],
                ins=[cc_in[h].ap().opt()],
                outs=[cc_out[h].ap().opt()],
            ).then_inc(s_cc, 1)
            gi = h // 2
            if gi in EARLY_G:
                g.wait_ge(s_cc, h + 1)
                for r, hh, ro in cc_rows(gi):
                    if hh != h:
                        continue
                    dt = RG * gi + r
                    g.dma_start(out=big2a[:, bsl(dt)],
                                in_=cc_out[h][ro:ro + 128, :]
                                ).then_inc(s_bl[gi], 16)

    ps3.close()
    p3.close()

    p4 = contextlib.ExitStack()
    big2b = p4.enter_context(nc.sbuf_tensor("big2b", [128, (ET - BG2A) * T], BF16))
    ps4 = contextlib.ExitStack()
    pss = [ps4.enter_context(nc.psum_tensor(f"pss{j}", [128, T], F32))
           for j in range(2)]

    def big2v(dt):
        if dt < BG2A:
            return big2a[:, bsl(dt)]
        return big2b[:, T * (dt - BG2A):T * (dt - BG2A) + T]

    @blk.scalar
    def _(s):
        for hh in range(NH):
            for p in range(2):
                s.wait_ge(s_ci[hh][p], 16 * 2 * DBF * CPP[(hh, p)])
        for gi in range(G):
            if gi in EARLY_G:
                continue
            for h in (2 * gi, 2 * gi + 1):
                s.wait_ge(s_cc, h + 1)
                for r, hh, ro in cc_rows(gi):
                    if hh != h:
                        continue
                    dt = RG * gi + r
                    s.dma_start(out=big2v(dt),
                                in_=cc_out[h][ro:ro + 128, :]
                                ).then_inc(s_bl[gi], 16)
        for ft in range(2, FT):
            s.wait_ge(s_c1, ft - 1)
            s.dma_start(out=gtr[:, (ft % 2) * T:(ft % 2) * T + T],
                        in_=g_d[128 * ft:128 * ft + 128, :]).then_inc(s_gl[ft % 2], 16)

    @blk.gpsimd
    def _(g):
        g.wait_ge(s_gs[0], 16 * ((FT + 1) // 2))
        g.wait_ge(s_gs[1], 16 * (FT // 2))
        for ft in range(FT):
            if ft >= 2:
                g.wait_ge(s_sm, ft - 1)
            g.dma_start(out=wor[:, (ft % 2) * WB:(ft % 2) * WB + WB],
                        in_=wo[ft, :, :]).then_inc(s_wo[ft % 2], 16)

    @blk.tensor
    def _(t):
        for ft in range(FT):
            sl = ft % 2
            t.wait_ge(s_wo[ft % 2], 16 * (ft // 2 + 1))
            if ft >= 2:
                t.wait_ge(s_c1, ft - 1)
            wof = sl * WB
            for dt in range(ET):
                if ft == 0 and dt % RG == 0:
                    t.wait_ge(s_bl[dt // RG], 16 * RG)
                b2 = big2v(dt)
                for c in range(NCC):
                    ins = t.matmul(
                        pss[sl][:, QCW * c:QCW * c + QCW],
                        wor[:, wof + 128 * dt:wof + 128 * dt + 128],
                        b2[:, QCW * c:QCW * c + QCW],
                        start=(dt == 0 and c % SPB == 0), stop=(dt == ET - 1),
                        skip_group_check=True)
                    if dt == ET - 1 and c == NCC - 1:
                        ins.then_inc(s_sm, 1)

    @blk.vector
    def _(v):
        for ft in range(FT):
            sl = slice((ft % 2) * T, (ft % 2) * T + T)
            v.wait_ge(s_sm, ft + 1)
            v.wait_ge(s_gl[ft % 2], 16 * (ft // 2 + 1))
            if ft >= 2:
                v.wait_ge(s_c2, ft - 1)
            v.tensor_tensor(two[:, sl], pss[ft % 2][:, :], gtr[:, sl],
                            ALU.mult).then_inc(s_c1, 1)
            v.wait_ge(s_xr[ft % 2], 16 * (ft // 2 + 1))
            v.wait_ge(s_c1, ft + 1)
            v.tensor_tensor(xrr[:, sl], two[:, sl], xrr[:, sl],
                            ALU.add).then_inc(s_c2, 1)

    @blk.sync
    def _(sy):
        for hh in range(NH):
            for p in range(2):
                sy.wait_ge(s_ci[hh][p], 16 * 2 * DBF * CPP[(hh, p)])
        for ft in range(FT):
            if ft >= 2:
                sy.wait_ge(s_c2, ft - 1)
                sy.dma_start(out=outT[128 * (ft - 2):128 * (ft - 2) + 128, :],
                             in_=xrr[:, (ft % 2) * T:(ft % 2) * T + T]
                             ).then_inc(s_st, 16)
                sy.wait_ge(s_st, 16 * (ft - 1))
                sy.dma_start(out=xrr[:, (ft % 2) * T:(ft % 2) * T + T],
                             in_=xresT[128 * ft:128 * ft + 128, :]
                             ).then_inc(s_xr[ft % 2], 16)
        for ft in (FT - 2, FT - 1):
            sy.wait_ge(s_c2, ft + 1)
            sy.dma_start(out=outT[128 * ft:128 * ft + 128, :],
                         in_=xrr[:, (ft % 2) * T:(ft % 2) * T + T]
                         ).then_inc(s_st, 16)
        sy.wait_ge(s_st, 16 * FT)

    ps4.close()
    p4.close()
    p4pre.close()
    ctx.close()
    return nc


def kernel(**inputs):
    cfg = REAL
    maps = make_inputs(inputs, cfg)
    nc = build_program(cfg, debug=False)
    res = run_bass_kernel_spmd(nc, maps, core_ids=list(range(8)))
    out = np.empty((4, cfg.T, cfg.E), dtype=np.float32)
    for c in range(8):
        b, h = c >> 1, c & 1
        out[b][:, cfg.FH * h:cfg.FH * h + cfg.FH] = res.results[c]["outT"].T
    return out


# version 21
# speedup vs baseline: 1.2153x; 1.2153x over previous
import sys
if "/opt/trn_rl_repo" not in sys.path:
    sys.path.insert(0, "/opt/trn_rl_repo")

import contextlib
import numpy as np
import ml_dtypes
import concourse.bass as bass
import concourse.mybir as mybir
from concourse.bass_utils import run_bass_kernel_spmd

F32 = mybir.dt.float32
BF16 = mybir.dt.bfloat16
AF = mybir.ActivationFunctionType
ALU = mybir.AluOpType
BB = ml_dtypes.bfloat16

HD = 256
ROT = 64
NQC = 4


class Cfg:
    def __init__(self, T=2048, E=4096, NH=8):
        self.T = T
        self.E = E
        self.NH = NH
        self.ET = E // 128
        self.DH = NH * HD
        self.DT = self.DH // 128
        self.FH = self.DH
        self.FT = self.FH // 128
        self.QCW = T // NQC
        self.KJQ = self.QCW // 128
        self.KJT = T // 128
        self.G = NH // 2
        assert self.FH * 2 == E and self.QCW % 128 == 0


REAL = Cfg()


def _pmaj(w, inner, ET):
    nblk = w.shape[1] // inner
    t = w.reshape(ET, 128, nblk, inner).transpose(2, 1, 0, 3)
    return np.ascontiguousarray(t.reshape(nblk, 128, ET * inner))


def make_inputs(inp, cfg=REAL):
    T, E, NH, ET, FT = cfg.T, cfg.E, cfg.NH, cfg.ET, cfg.FT
    x = np.asarray(inp["L0_attn_hidden_states"])
    Wq = np.asarray(inp["L0_attn_Wq"])
    Wk = np.asarray(inp["L0_attn_Wk"])
    Wv = np.asarray(inp["L0_attn_Wv"])
    Wo = np.asarray(inp["L0_attn_Wo"])
    Wf = np.asarray(inp["L0_ffn_weight"])
    bias = np.asarray(inp["L0_ffn_bias"])
    gw = np.asarray(inp["L0_ffn_gate_w"])
    ics = np.asarray(inp["L0_ffn_in_channel_scale"])
    ocs = np.asarray(inp["L0_ffn_out_channel_scale"])

    inv_freq = 1.0 / 10000.0 ** (np.arange(0, ROT, 2, dtype=np.float64) / ROT)
    sinu = np.outer(np.arange(T, dtype=np.float64), inv_freq)
    ctab = np.ascontiguousarray(np.repeat(np.cos(sinu).T, 2, axis=0)).astype(BB)
    stab = np.ascontiguousarray(np.repeat(np.sin(sinu).T, 2, axis=0)).astype(BB)
    a = np.arange(128)[:, None]
    col = np.arange(cfg.QCW)[None, :]
    masks = np.concatenate(
        [(128 * r + a <= col).astype(BB) for r in range(cfg.KJQ)], axis=1)

    NHT = 2 * NH
    perm = np.concatenate(
        [np.arange(HD * hh, HD * hh + HD)
         for g in range(cfg.G)
         for hh in (2 * g, 2 * g + 1, NHT // 2 + 2 * g, NHT // 2 + 2 * g + 1)])

    gw_p = np.ascontiguousarray(
        gw.T.reshape(ET, 128, 8).transpose(1, 0, 2).reshape(128, ET * 8))

    maps = []
    for c in range(8):
        b, h = c >> 1, c & 1
        fh = slice(cfg.FH * h, cfg.FH * h + cfg.FH)
        xt = np.ascontiguousarray(x[b].T)
        maps.append({
            "xT": xt,
            "xresT": np.ascontiguousarray(xt[fh]),
            "wq": _pmaj(np.ascontiguousarray(Wq[:, fh]), 128, ET),
            "wk": _pmaj(np.ascontiguousarray(Wk[:, fh]), 128, ET),
            "wv": _pmaj(np.ascontiguousarray(Wv[:, fh]), HD, ET),
            "wo": _pmaj(np.ascontiguousarray(Wo[perm][:, fh]), 128, ET),
            "wt": _pmaj(np.ascontiguousarray(Wf[fh, :].T), 128, ET),
            "gw": gw_p,
            "ics": np.ascontiguousarray(ics),
            "ocs": np.ascontiguousarray(ocs[:, fh]),
            "bias_t": np.ascontiguousarray(bias[fh].reshape(FT, 128).T),
            "ctab": ctab, "stab": stab, "masks": masks,
        })
    return maps


def build_program(cfg=REAL, debug=False, race_detect=True):
    T, E, NH = cfg.T, cfg.E, cfg.NH
    ET, DH, DT, FH, FT = cfg.ET, cfg.DH, cfg.DT, cfg.FH, cfg.FT
    QCW, KJQ, KJT, G = cfg.QCW, cfg.KJQ, cfg.KJT, cfg.G

    nc = bass.Bass("TRN2", target_bir_lowering=False,
                   detect_race_conditions=race_detect)

    din = lambda n, s, d=F32: nc.dram_tensor(n, s, d, kind="ExternalInput")
    xT = din("xT", [E, T])
    xresT = din("xresT", [FH, T])
    wq = din("wq", [DT, 128, ET * 128])
    wk = din("wk", [DT, 128, ET * 128])
    wv = din("wv", [NH, 128, ET * HD])
    wo = din("wo", [FT, 128, ET * 128])
    wt = din("wt", [FT, 128, ET * 128])
    gw = din("gw", [128, ET * 8])
    ics = din("ics", [8, E])
    ocs = din("ocs", [8, FH])
    bias_t = din("bias_t", [128, FT])
    ctab = din("ctab", [ROT, T], BF16)
    stab = din("stab", [ROT, T], BF16)
    masks = din("masks", [128, cfg.KJQ * QCW], BF16)

    outT = nc.dram_tensor("outT", [FH, T], F32, kind="ExternalOutput")

    spill_kind = "ExternalOutput" if debug else "Internal"
    q_d = nc.dram_tensor("q_d", [DH, T], BF16, kind=spill_kind)
    k_d = nc.dram_tensor("k_d", [DH, T], BF16, kind=spill_kind)
    v_d = nc.dram_tensor("v_d", [NH, T, HD], BF16, kind=spill_kind)
    g_d = nc.dram_tensor("g_d", [FH, T], BF16, kind=spill_kind)
    xm_tap = (nc.dram_tensor("xm_tap", [128, ET * T], BF16, kind="ExternalOutput")
              if debug else None)
    at_tap = (nc.dram_tensor("at_tap", [DH, T], BF16, kind="ExternalOutput")
              if debug else None)
    cc_in = [[nc.dram_tensor(f"cc_in{h}_{hf}", [HD, T // 2], BF16)
              for hf in range(2)] for h in range(NH)]
    cc_out = [[nc.dram_tensor(f"cc_out{h}_{hf}", [2 * HD, T // 2], BF16)
               for hf in range(2)] for h in range(NH)]

    ctx = contextlib.ExitStack()
    sem = lambda n: ctx.enter_context(nc.semaphore(n))

    s_pre = sem("s_pre")
    s_pm = sem("s_pm")
    s_x = [sem(f"s_x{i}") for i in range(4)]
    s_conv = sem("s_conv")
    s_sq = sem("s_sq")
    s_stm = sem("s_stm")
    s_fs = sem("s_fs")
    s_fv = sem("s_fv")
    s_bc = sem("s_bc")
    s_rbd = sem("s_rbd")
    s_nv = sem("s_nv")
    s_rt = sem("s_rt")
    s_wl = [sem(f"s_wl{i}") for i in range(2)]
    s_qkm = sem("s_qkm")
    s_qke = sem("s_qke")
    s_rope = sem("s_rope")
    s_spl = [sem(f"s_spl{i}") for i in range(2)]
    s_wv = [sem(f"s_wv{i}") for i in range(2)]
    s_wvh = sem("s_wvh")
    s_ics = sem("s_ics")
    s_vm = sem("s_vm")
    s_ve = sem("s_ve")
    s_vs = [sem(f"s_vs{i}") for i in range(4)]
    s_is = sem("s_is")
    s_ise = sem("s_ise")
    s_xmv = sem("s_xmv")
    s_ism = sem("s_ism")
    s_icm = sem("s_icm")
    s_sl = [sem(f"s_sl{i}") for i in range(2)]
    s_iseh = sem("s_iseh")
    s_xmg = sem("s_xmg")
    s_xms = sem("s_xms")
    s_wt = [sem(f"s_wt{i}") for i in range(2)]
    s_sgn = sem("s_sgn")
    s_osc = sem("s_osc")
    s_osce = sem("s_osce")
    s_gm = sem("s_gm")
    s_gt = sem("s_gt")
    s_sil = sem("s_sil")
    s_gs = [sem(f"s_gs{i}") for i in range(2)]
    s_qh = [sem(f"s_qh{i}") for i in range(2)]
    s_vhl = [sem(f"s_vhl{i}") for i in range(2)]
    s_L = sem("s_L")
    s_exp = sem("s_exp")
    s_msk = sem("s_msk")
    s_att = sem("s_att")
    s_dna = sem("s_dna")
    s_dnr = sem("s_dnr")
    s_dnc = sem("s_dnc")
    s_dnb = sem("s_dnb")
    s_ae = sem("s_ae")
    s_ci = [[sem(f"s_ci{i}_{p}") for p in range(2)] for i in range(NH)]
    s_cc = sem("s_cc")
    s_bl = [sem(f"s_bl{i}") for i in range(G)]
    s_wo = [sem(f"s_wo{i}") for i in range(2)]
    s_sm = sem("s_sm")
    s_gl = [sem(f"s_gl{i}") for i in range(2)]
    s_xr = [sem(f"s_xr{i}") for i in range(2)]
    s_c1 = sem("s_c1")
    s_c2 = sem("s_c2")
    s_st = sem("s_st")
    s_sse = sem("s_sse")
    s_vse = sem("s_vse")
    s_gse = sem("s_gse")

    class _Self:
        def __init__(self, s):
            self.s, self.n = s, 0

        def prod(self, ins):
            ins.then_inc(self.s, 1)
            self.n += 1
            return ins

        def cons(self, eng):
            eng.wait_ge(self.s, self.n)

    se_s, se_v, se_g = _Self(s_sse), _Self(s_vse), _Self(s_gse)

    blk = ctx.enter_context(nc.Block())

    ones_col = ctx.enter_context(nc.sbuf_tensor("ones_col", [128, 1], BF16))
    ones_sc = ctx.enter_context(nc.sbuf_tensor("ones_sc", [128, 1], BF16))
    ones_row = ctx.enter_context(nc.sbuf_tensor("ones_row", [1, 128], BF16))
    ones_dz = ctx.enter_context(nc.sbuf_tensor("ones_dz", [128, 128], BF16))
    ctab_sb = ctx.enter_context(nc.sbuf_tensor("ctab_sb", [ROT, T], BF16))
    stab_sb = ctx.enter_context(nc.sbuf_tensor("stab_sb", [ROT, T], BF16))
    routing = ctx.enter_context(nc.sbuf_tensor("routing", [128, T], BF16))
    gw_sb = ctx.enter_context(nc.sbuf_tensor("gw_sb", [128, ET * 8], BF16))
    ocs_sb = ctx.enter_context(nc.sbuf_tensor("ocs_sb", [128, FH], BF16))
    bias_sb = ctx.enter_context(nc.sbuf_tensor("bias_sb", [128, FT], F32))

    eps_sb = ctx.enter_context(nc.sbuf_tensor("eps_sb", [1, 1], F32))

    @blk.gpsimd
    def _(g):
        g.memset(ones_col[:, :], 1.0).then_inc(s_pm, 1)
        g.memset(ones_sc[:, :], 1.0 / E).then_inc(s_pm, 1)
        g.memset(ones_row[:, :], 1.0).then_inc(s_pm, 1)
        se_g.prod(g.memset(ones_dz[:, :], 0.0))
        se_g.cons(g)
        g.memset(ones_dz[:, 0:1], 1.0).then_inc(s_pm, 1)
        g.memset(ocs_sb[:, :], 0.0).then_inc(s_pm, 1)
        g.memset(routing[:, :], 0.0).then_inc(s_pm, 1)
        g.memset(eps_sb[:, :], 1e-5).then_inc(s_pm, 1)
        g.dma_start(out=ctab_sb[:, :], in_=ctab[:, :]).then_inc(s_pre, 16)
        g.dma_start(out=stab_sb[:, :], in_=stab[:, :]).then_inc(s_pre, 16)
        g.dma_start(out=gw_sb[:, :], in_=gw[:, :]).then_inc(s_pre, 16)
        g.wait_ge(s_pm, 5)
        g.dma_start(out=ocs_sb[0:8, :], in_=ocs[:, :]).then_inc(s_pre, 16)
        g.dma_start(out=bias_sb[:, :], in_=bias_t[:, :]).then_inc(s_pre, 16)

    PRE = 5 * 16
    PM = 7

    big_ctx = contextlib.ExitStack()
    big = big_ctx.enter_context(nc.sbuf_tensor("big", [128, ET * T], BF16))
    bsl = lambda i: slice(T * i, T * i + T)

    p0l = contextlib.ExitStack()
    r_bf = p0l.enter_context(nc.sbuf_tensor("r_bf", [1, T], BF16))
    mr_bf = p0l.enter_context(nc.sbuf_tensor("mr_bf", [1, T], BF16))
    rb_sb = p0l.enter_context(nc.sbuf_tensor("rb_sb", [128, T], BF16))
    mrb_sb = p0l.enter_context(nc.sbuf_tensor("mrb_sb", [128, T], BF16))
    tn_v = p0l.enter_context(nc.sbuf_tensor("tn_v", [128, T], BF16))
    tn_g = p0l.enter_context(nc.sbuf_tensor("tn_g", [128, T], BF16))
    p0 = contextlib.ExitStack()
    xsq = p0.enter_context(nc.sbuf_tensor("xsq", [128, 2 * T], BF16))
    b_row = p0.enter_context(nc.sbuf_tensor("b_row", [1, T], F32))
    msq_bf = p0.enter_context(nc.sbuf_tensor("msq_bf", [1, T], BF16))

    ps0 = contextlib.ExitStack()
    ps_sum = ps0.enter_context(nc.psum_tensor("ps_sum", [1, T], F32))
    ps_sq = ps0.enter_context(nc.psum_tensor("ps_sq", [1, T], F32))

    NCC = T // QCW
    SPB = max(1, 512 // QCW)

    @blk.gpsimd
    def _(g):
        for i in range(ET):
            if i >= 4:
                g.wait_ge(s_x[i % 4], 16 * (i // 4))
            g.dma_start(out=big[:, bsl(i)],
                        in_=xT[128 * i:128 * i + 128, :]).then_inc(s_x[i % 4], 16)

    @blk.vector
    def _(v):
        for i in range(ET):
            sl = slice((i % 2) * T, (i % 2) * T + T)
            v.wait_ge(s_x[i % 4], 16 * (i // 4 + 1))
            if i >= 2:
                v.wait_ge(s_stm, i - 1)
            v.tensor_tensor(xsq[:, sl], big[:, bsl(i)], big[:, bsl(i)],
                            ALU.mult).then_inc(s_sq, 1)

    @blk.tensor
    def _(t):
        t.wait_ge(s_pm, PM)
        for i in range(ET):
            t.wait_ge(s_x[i % 4], 16 * (i // 4 + 1))
            for c in range(NCC):
                t.matmul(ps_sum[0:1, QCW * c:QCW * c + QCW], ones_sc[:, :],
                         big[:, T * i + QCW * c:T * i + QCW * c + QCW],
                         start=(i == 0 and c % SPB == 0), stop=(i == ET - 1),
                         skip_group_check=True)
            t.wait_ge(s_sq, i + 1)
            for c in range(NCC):
                ins = t.matmul(
                    ps_sq[0:1, QCW * c:QCW * c + QCW], ones_sc[:, :],
                    xsq[:, (i % 2) * T + QCW * c:(i % 2) * T + QCW * c + QCW],
                    start=(i == 0 and c % SPB == 0), stop=(i == ET - 1),
                    skip_group_check=True)
                if c == NCC - 1:
                    ins.then_inc(s_stm, 1)

    @blk.scalar
    def _(s):
        s.wait_ge(s_stm, ET)
        s.wait_ge(s_pm, PM)
        s.activation(msq_bf[:, :], ps_sum[0:1, :], AF.Square).then_inc(s_fs, 1)

    @blk.vector
    def _(v):
        v.wait_ge(s_fs, 1)
        v.tensor_tensor(b_row[:, :], ps_sq[0:1, :], msq_bf[:, :],
                        ALU.subtract).then_inc(s_fv, 1)

    @blk.scalar
    def _(s):
        s.wait_ge(s_fv, 1)
        se_s.prod(s.activation(b_row[:, :], b_row[:, :], AF.Ln,
                               bias=eps_sb[:, :], scale=1.0))
        se_s.cons(s)
        s.activation(r_bf[:, :], b_row[:, :], AF.Exp,
                     scale=-0.5).then_inc(s_fs, 1)

    @blk.vector
    def _(v):
        v.wait_ge(s_fs, 2)
        v.tensor_tensor(mr_bf[:, :], ps_sum[0:1, :], r_bf[:, :],
                        ALU.mult).then_inc(s_fv, 1)

    ps0.close()
    p0.close()
    ps0b = contextlib.ExitStack()
    ps_rb = ps0b.enter_context(nc.psum_tensor("ps_rb", [128, T], F32))
    ps_mrb = ps0b.enter_context(nc.psum_tensor("ps_mrb", [128, T], F32))

    @blk.tensor
    def _(t):
        t.wait_ge(s_fv, 2)
        t.wait_ge(s_fs, 2)
        for c in range(NCC):
            t.matmul(ps_rb[:, QCW * c:QCW * c + QCW], ones_row[:, :],
                     r_bf[0:1, QCW * c:QCW * c + QCW], start=True, stop=True)
            ins = t.matmul(ps_mrb[:, QCW * c:QCW * c + QCW], ones_row[:, :],
                           mr_bf[0:1, QCW * c:QCW * c + QCW], start=True, stop=True)
            if c == NCC - 1:
                ins.then_inc(s_bc, 1)

    @blk.scalar
    def _(s):
        s.wait_ge(s_bc, 1)
        s.copy(rb_sb[:, :], ps_rb[:, :]).then_inc(s_rbd, 1)
        s.copy(mrb_sb[:, :], ps_mrb[:, :]).then_inc(s_rbd, 1)

    NGT = 0
    gp_tiles = list(range(ET - NGT, ET))
    vec_tiles = [i for i in range(ET) if i not in gp_tiles]

    p1r = contextlib.ExitStack()
    rl_exp = p1r.enter_context(nc.sbuf_tensor("rl_exp", [8, T], BF16))
    rs_bf = p1r.enter_context(nc.sbuf_tensor("rs_bf", [1, T], BF16))
    p1qk = contextlib.ExitStack()
    WB = ET * 128
    wring = p1qk.enter_context(nc.sbuf_tensor("wring", [128, 2 * WB], BF16))
    qbf = p1qk.enter_context(nc.sbuf_tensor("qbf", [128, 2 * T], BF16))
    qs = p1qk.enter_context(nc.sbuf_tensor("qs", [ROT, T], BF16))

    norm_seq = []
    iv = ig = 0
    while iv < len(vec_tiles) or ig < len(gp_tiles):
        tv = 5.2 * (iv + 1) if iv < len(vec_tiles) else 1e9
        tg = 9.6 * (ig + 1) if ig < len(gp_tiles) else 1e9
        if tv <= tg:
            norm_seq.append((vec_tiles[iv], "v", iv + 1))
            iv += 1
        else:
            norm_seq.append((gp_tiles[ig], "g", ig + 1))
            ig += 1

    @blk.gpsimd
    def _(g):
        g.wait_ge(s_stm, ET)
        g.wait_ge(s_fv, 2)
        g.wait_ge(s_fs, 2)
        for j in range(2):
            g.dma_start(out=wring[:, (j % 2) * WB:(j % 2) * WB + WB],
                        in_=wq[j, :, :]).then_inc(s_wl[j % 2], 16)

    @blk.vector
    def _(v):
        v.wait_ge(s_rbd, 1)
        for n, i in enumerate(vec_tiles):
            if n >= 1:
                v.wait_ge(s_nv, n)
            se_v.prod(v.tensor_tensor(tn_v[:, :], big[:, bsl(i)], rb_sb[:, :],
                                      ALU.mult))
            se_v.cons(v)
            if n == 0:
                v.wait_ge(s_rbd, 2)
            v.tensor_tensor(big[:, bsl(i)], tn_v[:, :], mrb_sb[:, :],
                            ALU.subtract).then_inc(s_nv, 1)

    if gp_tiles:
        @blk.gpsimd
        def _(g):
            g.wait_ge(s_rbd, 2)
            for n, i in enumerate(gp_tiles):
                if n >= 1:
                    g.wait_ge(s_ng, n)
                se_g.prod(g.tensor_tensor(tn_g[:, :], big[:, bsl(i)], rb_sb[:, :],
                                          ALU.mult))
                se_g.cons(g)
                g.tensor_tensor(big[:, bsl(i)], tn_g[:, :], mrb_sb[:, :],
                                ALU.subtract).then_inc(s_ng, 1)

    ps0b.close()

    ps1 = contextlib.ExitStack()
    psq = [ps1.enter_context(nc.psum_tensor(f"psq{j}", [128, T], F32))
           for j in range(2)]
    ps_rl = psq[1][0:8, :]
    ps_rs = psq[1][0:1, :]
    ps_rbc = psq[1][0:8, :]

    @blk.tensor
    def _(t):
        t.wait_ge(s_pre, PRE)
        first_e = norm_seq[0][0]
        last_e = norm_seq[-1][0]
        for i, which, cnt in norm_seq:
            if which == "v":
                t.wait_ge(s_nv, cnt)
            else:
                t.wait_ge(s_ng, cnt)
            for c in range(NCC):
                ins = t.matmul(ps_rl[:, QCW * c:QCW * c + QCW],
                               gw_sb[:, 8 * i:8 * i + 8],
                               big[:, T * i + QCW * c:T * i + QCW * c + QCW],
                               start=(i == first_e and c % SPB == 0),
                               stop=(i == last_e), skip_group_check=True)
                if i == last_e and c == NCC - 1:
                    ins.then_inc(s_rt, 1)
            if i == first_e:
                t.wait_ge(s_wl[0], 16)
            for c in range(NCC):
                ins = t.matmul(psq[0][:, QCW * c:QCW * c + QCW],
                               wring[:, 128 * i:128 * i + 128],
                               big[:, T * i + QCW * c:T * i + QCW * c + QCW],
                               start=(i == first_e and c % SPB == 0),
                               stop=(i == last_e), skip_group_check=True)
                if i == last_e and c == NCC - 1:
                    ins.then_inc(s_qkm, 1)

    @blk.scalar
    def _(s):
        s.wait_ge(s_rt, 1)
        s.activation(rl_exp[:, :], ps_rl[:, :], AF.Exp).then_inc(s_rt, 1)

    @blk.tensor
    def _(t):
        t.wait_ge(s_rt, 2)
        for c in range(NCC):
            ins = t.matmul(ps_rs[:, QCW * c:QCW * c + QCW], ones_col[0:8, :],
                           rl_exp[:, QCW * c:QCW * c + QCW], start=True, stop=True)
            if c == NCC - 1:
                ins.then_inc(s_rt, 1)

    @blk.scalar
    def _(s):
        s.wait_ge(s_rt, 3)
        se_s.prod(s.activation(ps_rs[:, :], ps_rs[:, :], AF.Ln))
        se_s.cons(s)
        s.activation(rs_bf[:, :], ps_rs[:, :], AF.Exp,
                     scale=-1.0).then_inc(s_rt, 2)

    @blk.tensor
    def _(t):
        t.wait_ge(s_rt, 5)
        for c in range(NCC):
            ins = t.matmul(ps_rbc[:, QCW * c:QCW * c + QCW], ones_row[0:1, 0:8],
                           rs_bf[0:1, QCW * c:QCW * c + QCW], start=True, stop=True)
            if c == NCC - 1:
                ins.then_inc(s_rt, 1)

    @blk.vector
    def _(v):
        v.wait_ge(s_rt, 6)
        v.tensor_tensor(routing[0:8, :], rl_exp[:, :], ps_rbc[:, :],
                        ALU.mult).then_inc(s_rt, 1)

    SWAP = []
    for i in range(16):
        SWAP += [2 * i + 1, 2 * i]

    @blk.gpsimd
    def _(g):
        for j in range(2, 2 * DT):
            g.wait_ge(s_qkm, j - 1)
            src = wq if j < DT else wk
            g.dma_start(out=wring[:, (j % 2) * WB:(j % 2) * WB + WB],
                        in_=src[j % DT, :, :]).then_inc(s_wl[j % 2], 16)

    @blk.tensor
    def _(t):
        for j in range(1, 2 * DT):
            if j == 1:
                t.wait_ge(s_rt, 7)
            t.wait_ge(s_wl[j % 2], 16 * (j // 2 + 1))
            if j >= 2:
                t.wait_ge(s_qke, j - 1)
            wof = (j % 2) * WB
            for e in range(ET):
                for c in range(NCC):
                    ins = t.matmul(
                        psq[j % 2][:, QCW * c:QCW * c + QCW],
                        wring[:, wof + 128 * e:wof + 128 * e + 128],
                        big[:, T * e + QCW * c:T * e + QCW * c + QCW],
                        start=(e == 0 and c % SPB == 0), stop=(e == ET - 1),
                        skip_group_check=True)
                    if e == ET - 1 and c == NCC - 1:
                        ins.then_inc(s_qkm, 1)


    @blk.scalar
    def _(s):
        for j in range(2 * DT):
            s.wait_ge(s_qkm, j + 1)
            if j >= 2:
                s.wait_ge(s_spl[j % 2], 16 * (j // 2))
            s.activation(qbf[:, (j % 2) * T:(j % 2) * T + T], psq[j % 2][:, :],
                         AF.Copy).then_inc(s_qke, 1)

    @blk.vector
    def _(v):
        v.wait_ge(s_pre, PRE)
        nrope_v = 0
        for j in range(2 * DT):
            if (j % DT) % 2 != 0:
                continue
            sl = slice((j % 2) * T, (j % 2) * T + T)
            v.wait_ge(s_qke, j + 1)
            if nrope_v >= 1:
                v.wait_ge(s_rope, nrope_v)
            nrope_v += 1
            se_v.prod(v.stream_shuffle(qs[:, :], qbf[0:ROT, sl], SWAP))
            se_v.cons(v)
            se_v.prod(v.tensor_tensor(qbf[0:ROT, sl], qbf[0:ROT, sl],
                                      ctab_sb[:, :], ALU.mult))
            se_v.cons(v)
            se_v.prod(v.tensor_tensor(qs[:, :], qs[:, :], stab_sb[:, :],
                                      ALU.mult))
            se_v.cons(v)
            v.tensor_tensor(qbf[0:ROT, sl], qbf[0:ROT, sl], qs[:, :],
                            ALU.add).then_inc(s_rope, 1)

    @blk.sync
    def _(sy):
        nrope = 0
        for j in range(2 * DT):
            dt = j % DT
            if dt % 2 == 0:
                nrope += 1
                sy.wait_ge(s_rope, nrope)
            else:
                sy.wait_ge(s_qke, j + 1)
            dst = q_d if j < DT else k_d
            sy.dma_start(out=dst[128 * dt:128 * dt + 128, :],
                         in_=qbf[:, (j % 2) * T:(j % 2) * T + T]
                         ).then_inc(s_spl[j % 2], 16)

    ps1.close()
    p1qk.close()
    p1r.close()
    p0l.close()

    p1bv = contextlib.ExitStack()
    ics_sb = p1bv.enter_context(nc.sbuf_tensor("ics_sb", [128, E], BF16))
    p1v = contextlib.ExitStack()
    VB = ET * HD
    wvb = p1v.enter_context(nc.sbuf_tensor("wvb", [128, 2 * VB], BF16))
    vev = p1v.enter_context(nc.sbuf_tensor("vev", [128, 4 * HD], BF16))
    ps1v = contextlib.ExitStack()
    ps_big = ps1v.enter_context(nc.psum_tensor("ps_big", [128, 2 * T], F32))
    BW = 512
    psv = [ps_big[:, 0:HD], ps_big[:, BW:BW + HD]]
    ps_ish = [ps_big[:, T + (T // 2) * k:T + (T // 2) * (k + 1)]
              for k in range(2)]
    psg = [ps_big[:, T * j:T * j + T] for j in range(2)]

    @blk.gpsimd
    def _(g):
        g.wait_ge(s_qkm, 2 * DT)
        for kq in range(4):
            g.dma_start(out=wvb[:, kq * VB // 8:(kq + 1) * VB // 8],
                        in_=wv[0, :, kq * VB // 8:(kq + 1) * VB // 8]
                        ).then_inc(s_wvh, 16)
        g.wait_ge(s_spl[0], 16 * DT)
        g.wait_ge(s_spl[1], 16 * DT)
        g.dma_start(out=wvb[:, VB // 2:VB],
                    in_=wv[0, :, VB // 2:VB]).then_inc(s_wvh, 16)
        for h in range(1, NH):
            if h >= 2:
                g.wait_ge(s_vm, KJT * (h - 1))
            g.dma_start(out=wvb[:, (h % 2) * VB:(h % 2) * VB + VB],
                        in_=wv[h, :, :]).then_inc(s_wv[h % 2], 16)
        g.memset(ics_sb[:, :], 0.0).then_inc(s_icm, 1)
        g.wait_ge(s_icm, 1)
        g.dma_start(out=ics_sb[0:8, :], in_=ics[:, :]).then_inc(s_ics, 16)

    @blk.tensor
    def _(t):
        for h in range(NH):
            if h == 0:
                t.wait_ge(s_qke, 2 * DT)
            elif h % 2 == 0:
                t.wait_ge(s_wv[0], 16 * (h // 2))
            else:
                t.wait_ge(s_wv[1], 16 * (h // 2 + 1))
            wof = (h % 2) * VB
            for ti in range(KJT):
                gi = KJT * h + ti
                if gi >= 2:
                    t.wait_ge(s_ve, gi - 1)
                for e in range(ET):
                    if h == 0 and ti == 0 and e % 4 == 0 and e < ET // 2:
                        t.wait_ge(s_wvh, 16 * (e // 4 + 1))
                    if h == 0 and ti == 0 and e == ET // 2:
                        t.wait_ge(s_wvh, 16 * 5)
                    ins = t.matmul(
                        psv[gi % 2][:, :],
                        big[:, T * e + 128 * ti:T * e + 128 * ti + 128],
                        wvb[:, wof + HD * e:wof + HD * e + HD],
                        start=(e == 0), stop=(e == ET - 1), skip_group_check=True)
                    if e == ET - 1:
                        ins.then_inc(s_vm, 1)

    @blk.scalar
    def _(s):
        for gi in range(KJT * NH):
            s.wait_ge(s_vm, gi + 1)
            if gi >= 4:
                s.wait_ge(s_vs[gi % 4], 16 * (gi // 4))
            s.activation(vev[:, (gi % 4) * HD:(gi % 4) * HD + HD],
                         psv[gi % 2][:, :], AF.Copy).then_inc(s_ve, 1)
            if gi % 2 == 1:
                h, ti = gi // KJT, gi % KJT
                s.wait_ge(s_ve, gi + 1)
                s.dma_start(
                    out=v_d[h, 128 * ti:128 * ti + 128, :],
                    in_=vev[:, (gi % 4) * HD:(gi % 4) * HD + HD]
                ).then_inc(s_vs[gi % 4], 16)

    @blk.sync
    def _(sy):
        for h in range(NH):
            for ti in range(KJT):
                gi = KJT * h + ti
                if gi % 2 == 1:
                    continue
                sy.wait_ge(s_ve, gi + 1)
                sy.dma_start(
                    out=v_d[h, 128 * ti:128 * ti + 128, :],
                    in_=vev[:, (gi % 4) * HD:(gi % 4) * HD + HD]
                ).then_inc(s_vs[gi % 4], 16)

    p1v.close()

    NGX = max(round(ET * 0.25), 1)
    NVX = ET - NGX

    ISQ = []
    _iv = _ig = 0
    while _iv < NVX or _ig < NGX:
        _tv = 2.6 * (_iv + 1) if _iv < NVX else 1e9
        _tg = 4.8 * (_ig + 1) if _ig < NGX else 1e9
        if _tv <= _tg:
            ISQ.append(_iv)
            _iv += 1
        else:
            ISQ.append(NVX + _ig)
            _ig += 1
    NR = 2 * ET
    pos_of = {t_: p_ for p_, t_ in enumerate(ISQ)}
    evac_cnt = []
    vec_sl_cnt = []
    _n = 0
    _v = [0, 0]
    for r_ in range(NR):
        if ISQ[r_ // 2] >= NVX:
            _n += 1
        else:
            _v[r_ % 2] += 1
        evac_cnt.append(_n)
        vec_sl_cnt.append(_v[r_ % 2])

    def slot_wait(eng, pr):
        if ISQ[pr // 2] < NVX:
            eng.wait_ge(s_sl[pr % 2], vec_sl_cnt[pr])
        else:
            eng.wait_ge(s_iseh, evac_cnt[pr])

    def xm_wait(eng, i):
        if i < NVX:
            p_ = pos_of[i]
            eng.wait_ge(s_sl[0], vec_sl_cnt[2 * p_])
            eng.wait_ge(s_sl[1], vec_sl_cnt[2 * p_ + 1])
        else:
            eng.wait_ge(s_xmg, i - NVX + 1)

    p2 = contextlib.ExitStack()
    wtr = p2.enter_context(nc.sbuf_tensor("wtr", [128, 2 * WB], BF16))
    is_sb = p2.enter_context(nc.sbuf_tensor("is_sb", [128, 2 * T], BF16))
    oscale = p2.enter_context(nc.sbuf_tensor("oscale", [128, 2 * T], BF16))
    tsb = p2.enter_context(nc.sbuf_tensor("tsb", [128, 2 * T], BF16))
    gbf = p2.enter_context(nc.sbuf_tensor("gbf", [128, 2 * T], BF16))

    @blk.gpsimd
    def _(g):
        g.wait_ge(s_vm, KJT * (NH - 1))
        g.dma_start(out=wtr[:, 0:WB], in_=wt[0, :, :]).then_inc(s_wt[0], 16)
        for _k in range(4):
            g.wait_ge(s_vs[_k], 16 * (KJT * NH // 4))
        g.dma_start(out=wtr[:, WB:2 * WB], in_=wt[1, :, :]).then_inc(s_wt[1], 16)
        gp_idx = 0
        for p, i in enumerate(ISQ):
            if i < NVX:
                continue
            g.wait_ge(s_iseh, evac_cnt[2 * p + 1])
            g.tensor_tensor(big[:, bsl(i)], big[:, bsl(i)],
                            is_sb[:, (gp_idx % 2) * T:(gp_idx % 2) * T + T],
                            ALU.mult).then_inc(s_xmg, 1)
            gp_idx += 1
        for ot in range(2, FT):
            g.wait_ge(s_gm, ot - 1)
            g.dma_start(out=wtr[:, (ot % 2) * WB:(ot % 2) * WB + WB],
                        in_=wt[ot, :, :]).then_inc(s_wt[ot % 2], 16)

    @blk.vector
    def _(v):
        for p, i in enumerate(ISQ):
            if i >= NVX:
                continue
            for half in range(2):
                r = 2 * p + half
                hsl = slice(T * i + half * (T // 2),
                            T * i + half * (T // 2) + T // 2)
                v.wait_ge(s_ism, r + 1)
                v.tensor_tensor(big[:, hsl], big[:, hsl],
                                ps_ish[r % 2][:, :],
                                ALU.mult).then_inc(s_sl[r % 2], 1)

    if debug:
        @blk.sync
        def _(sy):
            for i in range(ET):
                xm_wait(sy, i)
                sy.dma_start(out=xm_tap[:, bsl(i)],
                             in_=big[:, bsl(i)]).then_inc(s_xms, 16)

    @blk.scalar
    def _(s):
        s.wait_ge(s_pre, PRE)
        s.wait_ge(s_wt[0], 16)
        s.sign(wtr[:, 0:WB], wtr[:, 0:WB]).then_inc(s_sgn, 1)
        s.wait_ge(s_osc, 1)
        s.copy(oscale[:, 0:T], psg[0][:, :]).then_inc(s_osce, 1)
        gp_idx = 0
        for r in range(NR):
            p = r // 2
            if ISQ[p] < NVX:
                continue
            s.wait_ge(s_ism, r + 1)
            if gp_idx >= 4 and r % 2 == 0:
                s.wait_ge(s_xmg, gp_idx // 2 - 1)
            s.activation(is_sb[:, (gp_idx // 2 % 2) * T + (r % 2) * (T // 2):
                                 (gp_idx // 2 % 2) * T + (r % 2) * (T // 2)
                                 + T // 2],
                         ps_ish[r % 2][:, :], AF.Copy).then_inc(s_iseh, 1)
            gp_idx += 1
        for ot in range(1, FT):
            s.wait_ge(s_wt[ot % 2], 16 * (ot // 2 + 1))
            if ot >= 2:
                s.wait_ge(s_gm, ot - 1)
            sl = slice((ot % 2) * WB, (ot % 2) * WB + WB)
            s.sign(wtr[:, sl], wtr[:, sl]).then_inc(s_sgn, 1)
            s.wait_ge(s_osc, ot + 1)
            s.copy(oscale[:, (ot % 2) * T:(ot % 2) * T + T],
                   psg[ot % 2][:, :]).then_inc(s_osce, 1)
            o = ot - 1
            s.wait_ge(s_gt, o + 1)
            if o >= 2:
                s.wait_ge(s_gs[o % 2], 16 * (o // 2))
            s.activation(gbf[:, (o % 2) * T:(o % 2) * T + T],
                         tsb[:, (o % 2) * T:(o % 2) * T + T], AF.Silu,
                         bias=bias_sb[:, o:o + 1], scale=1.0).then_inc(s_sil, 1)
        o = FT - 1
        s.wait_ge(s_gt, o + 1)
        s.wait_ge(s_gs[o % 2], 16 * (o // 2))
        s.activation(gbf[:, (o % 2) * T:(o % 2) * T + T],
                     tsb[:, (o % 2) * T:(o % 2) * T + T], AF.Silu,
                     bias=bias_sb[:, o:o + 1], scale=1.0).then_inc(s_sil, 1)

    @blk.tensor
    def _(t):
        t.wait_ge(s_ics, 16)
        t.wait_ge(s_icm, 1)
        t.wait_ge(s_ve, KJT * NH)
        for _k in range(4):
            t.wait_ge(s_vs[_k], 16 * (KJT * NH // 4))
        t.wait_ge(s_sgn, 1)
        for c in range(NCC):
            ins = t.matmul(psg[0][:, QCW * c:QCW * c + QCW],
                           ocs_sb[:, 0:128],
                           routing[:, QCW * c:QCW * c + QCW],
                           start=True, stop=True, skip_group_check=True)
            if c == NCC - 1:
                ins.then_inc(s_osc, 1)
        t.wait_ge(s_osce, 1)

        def g_mms(t, idx):
            i = ISQ[idx]
            xm_wait(t, i)
            for c in range(NCC):
                ins = t.matmul(
                    psg[0][:, QCW * c:QCW * c + QCW],
                    wtr[:, 128 * i:128 * i + 128],
                    big[:, T * i + QCW * c:T * i + QCW * c + QCW],
                    start=(idx == 0 and c % SPB == 0), stop=(idx == ET - 1),
                    skip_group_check=True)
                if idx == ET - 1 and c == NCC - 1:
                    ins.then_inc(s_gm, 1)

        for r in range(NR):
            i = ISQ[r // 2]
            half = r % 2
            if r >= 2:
                slot_wait(t, r - 2)
            for c in range(NCC // 2):
                cc = half * (NCC // 2) + c
                ins = t.matmul(ps_ish[r % 2][:, QCW * c:QCW * c + QCW],
                               ics_sb[:, 128 * i:128 * i + 128],
                               routing[:, QCW * cc:QCW * cc + QCW],
                               start=True, stop=True, skip_group_check=True)
                if c == NCC // 2 - 1:
                    ins.then_inc(s_ism, 1)
            if r % 2 == 1 and r // 2 >= 2:
                g_mms(t, r // 2 - 2)
        g_mms(t, ET - 2)
        g_mms(t, ET - 1)
        def osc_mms(t, ot):
            for c in range(NCC):
                ins = t.matmul(psg[ot % 2][:, QCW * c:QCW * c + QCW],
                               ocs_sb[:, 128 * ot:128 * ot + 128],
                               routing[:, QCW * c:QCW * c + QCW],
                               start=True, stop=True, skip_group_check=True)
                if c == NCC - 1:
                    ins.then_inc(s_osc, 1)

        for ot in range(1, FT):
            sl = ot % 2
            t.wait_ge(s_sgn, ot + 1)
            if ot == 1:
                slot_wait(t, NR - 2)
                slot_wait(t, NR - 1)
                osc_mms(t, 1)
            t.wait_ge(s_osce, ot + 1)
            wof = sl * WB
            for ht in range(ET):
                if ht == 6 and ot + 1 < FT:
                    t.wait_ge(s_gt, ot)
                    osc_mms(t, ot + 1)
                for c in range(NCC):
                    ins = t.matmul(
                        psg[sl][:, QCW * c:QCW * c + QCW],
                        wtr[:, wof + 128 * ht:wof + 128 * ht + 128],
                        big[:, T * ht + QCW * c:T * ht + QCW * c + QCW],
                        start=(ht == 0 and c % SPB == 0), stop=(ht == ET - 1),
                        skip_group_check=True)
                    if ht == ET - 1 and c == NCC - 1:
                        ins.then_inc(s_gm, 1)

    @blk.vector
    def _(v):
        for ot in range(FT):
            sl = slice((ot % 2) * T, (ot % 2) * T + T)
            v.wait_ge(s_gm, ot + 1)
            v.tensor_tensor(tsb[:, sl], psg[ot % 2][:, :], oscale[:, sl],
                            ALU.mult).then_inc(s_gt, 1)

    @blk.sync
    def _(sy):
        for ot in range(FT):
            sy.wait_ge(s_sil, ot + 1)
            sy.dma_start(out=g_d[128 * ot:128 * ot + 128, :],
                         in_=gbf[:, (ot % 2) * T:(ot % 2) * T + T]
                         ).then_inc(s_gs[ot % 2], 16)

    ps1v.close()
    p2.close()
    p1bv.close()
    big_ctx.close()

    BG2A = min(8, max(ET // 4, 1))
    p4pre = contextlib.ExitStack()
    wor = p4pre.enter_context(nc.sbuf_tensor("wor", [128, 2 * WB], BF16))
    gtr = p4pre.enter_context(nc.sbuf_tensor("gtr", [128, 2 * T], BF16))
    xrr = p4pre.enter_context(nc.sbuf_tensor("xrr", [128, 2 * T], F32))
    two = p4pre.enter_context(nc.sbuf_tensor("two", [128, 2 * T], F32))
    big2a = p4pre.enter_context(nc.sbuf_tensor("big2a", [128, BG2A * T], BF16))
    RG = 4 * HD // 128
    EARLY_G = [gi for gi in range(G) if RG * (gi + 1) <= BG2A]

    p3 = contextlib.ExitStack()
    qh = p3.enter_context(nc.sbuf_tensor("qh", [128, 2 * 2 * T], BF16))
    masks_sb = p3.enter_context(nc.sbuf_tensor("masks_sb", [128, KJQ * QCW], BF16))
    kh = p3.enter_context(nc.sbuf_tensor("kh", [128, 2 * 2 * T], BF16))
    vh = p3.enter_context(nc.sbuf_tensor("vh", [128, 2 * KJT * HD], BF16))
    pT = p3.enter_context(nc.sbuf_tensor("pT", [128, 2 * KJT * QCW], BF16))
    den_f = p3.enter_context(nc.sbuf_tensor("den_f", [1, QCW], F32))
    den_bf = p3.enter_context(nc.sbuf_tensor("den_bf", [1, QCW], BF16))
    rb_at = p3.enter_context(nc.sbuf_tensor("rb_at", [128, QCW], BF16))
    att_ev = p3.enter_context(nc.sbuf_tensor("att_ev", [128, 4 * QCW], BF16))
    ps3 = contextlib.ExitStack()
    psL = [ps3.enter_context(nc.psum_tensor(f"psL{j}", [128, QCW], F32))
           for j in range(4)]
    psO = [ps3.enter_context(nc.psum_tensor(f"psO{j}", [128, QCW], F32))
           for j in range(2)]
    ps_den = ps3.enter_context(nc.psum_tensor("ps_den", [128, QCW], F32))
    ps_drb = ps3.enter_context(nc.psum_tensor("ps_drb", [128, QCW], F32))

    chunks = [(h, qc) for h in range(NH) for qc in range(NQC)]
    nkj = lambda qc: KJQ * (qc + 1)
    cum_L = [0]
    for (_, qc) in chunks:
        cum_L.append(cum_L[-1] + nkj(qc))
    NCH = len(chunks)
    EXPI = []
    blk2ei = {}
    for ci, (_h, _qc) in enumerate(chunks):
        for kj in range(nkj(_qc)):
            blk2ei[cum_L[ci] + kj] = len(EXPI) + 1
            EXPI.append((ci, kj, 1))
    ci_ord = {}
    _cnt = {}
    for ci, (_h, _qc) in enumerate(chunks):
        key = (_h, ci % 2)
        _cnt[key] = _cnt.get(key, 0) + 1
        ci_ord[ci] = _cnt[key]
    CPP = {k: v for k, v in _cnt.items()}

    def cc_rows(gi):
        out = []
        for r in range(RG):
            h = 2 * gi + ((r // 2) % 2)
            out.append((r, h, 128 * (r % 2) + HD * (r // 4)))
        return out
    ei_chunk_done = [0] * (NCH + 1)
    blk_off = {}
    for idx, (ci, kj0, w) in enumerate(EXPI):
        ei_chunk_done[ci + 1] = idx + 1
        for d in range(w):
            blk_off[cum_L[ci] + kj0 + d] = d
    DBF = 2 if debug else 1
    CI_TOT = 16 * 2 * NCH * DBF
    PTW = KJT * QCW
    VHW = KJT * HD

    def emit_qk_load(sy, h):
        hb = h % 2
        sy.dma_start(out=qh[:, hb * 2 * T:hb * 2 * T + T],
                     in_=q_d[HD * h:HD * h + 128, :]).then_inc(s_qh[hb], 16)
        sy.dma_start(out=qh[:, hb * 2 * T + T:hb * 2 * T + 2 * T],
                     in_=q_d[HD * h + 128:HD * h + HD, :]).then_inc(s_qh[hb], 16)
        sy.dma_start(out=kh[:, hb * 2 * T:hb * 2 * T + T],
                     in_=k_d[HD * h:HD * h + 128, :]).then_inc(s_qh[hb], 16)
        sy.dma_start(out=kh[:, hb * 2 * T + T:hb * 2 * T + 2 * T],
                     in_=k_d[HD * h + 128:HD * h + HD, :]).then_inc(s_qh[hb], 16)

    def emit_v_load(sy, h):
        hb = h % 2
        sy.dma_start(
            out=vh[:, hb * VHW:hb * VHW + VHW],
            in_=v_d[h].rearrange("(t p) d -> p t d", p=128)).then_inc(s_vhl[hb], 16)

    def emit_head_load(sy, h):
        emit_qk_load(sy, h)
        emit_v_load(sy, h)

    def emit_den(t, ci):
        h, qc = chunks[ci]
        cb = ci % 2
        n = nkj(qc)
        t.wait_ge(s_vhl[h % 2], 16 * (h // 2 + 1))
        t.wait_ge(s_exp, ei_chunk_done[ci + 1])
        t.wait_ge(s_msk, KJQ * (ci + 1))
        if ci >= 1:
            t.wait_ge(s_dnr, ci)
            t.wait_ge(s_ae, 2 * ci)
        hb = h % 2
        for kj in range(n):
            ins = t.matmul(ps_den[:, :], ones_dz[:, :],
                           pT[:, cb * PTW + QCW * kj:cb * PTW + QCW * kj + QCW],
                           start=(kj == 0), stop=(kj == n - 1),
                           skip_group_check=True)
            if kj == n - 1:
                ins.then_inc(s_dna, 1)
        for dt in range(2):
            for kj in range(n):
                t.matmul(psO[dt][:, :],
                         vh[:, hb * VHW + HD * kj + 128 * dt:
                            hb * VHW + HD * kj + 128 * dt + 128],
                         pT[:, cb * PTW + QCW * kj:cb * PTW + QCW * kj + QCW],
                         start=(kj == 0), stop=(kj == n - 1),
                         skip_group_check=True)

    def emit_av2(t, ci):
        t.wait_ge(s_dnr, ci + 1)
        if ci >= 1:
            t.wait_ge(s_dnb, ci)
        t.matmul(ps_drb[:, :], ones_row[:, :], den_bf[0:1, :],
                 start=True, stop=True).then_inc(s_att, 1)

    @blk.tensor
    def _(t):
        t.wait_ge(s_gt, FT)
        for ci, (h, qc) in enumerate(chunks):
            hb = h % 2
            if qc == 0:
                t.wait_ge(s_qh[h % 2], 16 * 4 * (h // 2 + 1))
            for kj in range(nkj(qc)):
                li = cum_L[ci] + kj
                ei = blk2ei[li]
                if ei > 4:
                    t.wait_ge(s_exp, ei - 4)
                for dt in range(2):
                    ins = t.matmul(
                        psL[(ei - 1) % 4][:, 0:QCW],
                        kh[:, hb * 2 * T + dt * T + 128 * kj:
                           hb * 2 * T + dt * T + 128 * kj + 128],
                        qh[:, hb * 2 * T + dt * T + QCW * qc:
                           hb * 2 * T + dt * T + QCW * qc + QCW],
                        start=(dt == 0), stop=(dt == 1), skip_group_check=True)
                    if dt == 1:
                        ins.then_inc(s_L, 1)
            if ci > 0:
                emit_den(t, ci - 1)
                emit_av2(t, ci - 1)
        emit_den(t, NCH - 1)
        emit_av2(t, NCH - 1)

    @blk.scalar
    def _(s):
        s.wait_ge(s_gm, FT)
        s.wait_ge(s_gs[0], 16)
        s.wait_ge(s_gs[1], 16)
        if debug:
            s.wait_ge(s_xms, 16 * ET)
        for ft in range(2):
            s.dma_start(out=gtr[:, (ft % 2) * T:(ft % 2) * T + T],
                        in_=g_d[128 * ft:128 * ft + 128, :]).then_inc(s_gl[ft % 2], 16)

    def emit_recip(s, ci):
        s.wait_ge(s_dna, ci + 1)
        s.wait_ge(s_att, ci - 1 if ci >= 1 else 0)
        se_s.prod(s.activation(den_f[:, :], ps_den[0:1, :], AF.Ln))
        se_s.cons(s)
        s.activation(den_bf[:, :], den_f[:, :], AF.Exp,
                     scale=-1.0).then_inc(s_dnr, 1)

    @blk.scalar
    def _(s):
        s.wait_ge(s_gs[0], 16 * ((FT + 1) // 2))
        s.wait_ge(s_gs[1], 16 * (FT // 2))
        for idx, (ci, kj0, w) in enumerate(EXPI):
            h, qc = chunks[ci]
            cb = ci % 2
            li = cum_L[ci] + kj0
            if kj0 == 0 and ci >= 2:
                s.wait_ge(s_att, ci - 1)
            s.wait_ge(s_L, li + w)
            s.activation(pT[:, cb * PTW + QCW * kj0:
                            cb * PTW + QCW * (kj0 + w)],
                         psL[idx % 4][:, 0:QCW * w], AF.Exp,
                         scale=0.0625).then_inc(s_exp, 1)
            if kj0 + w == nkj(qc) and ci >= 1:
                emit_recip(s, ci - 1)
        emit_recip(s, NCH - 1)

    @blk.vector
    def _(v):
        v.wait_ge(s_pre, PRE + 16)
        for ci, (h, qc) in enumerate(chunks):
            cb = ci % 2
            for r in range(KJQ):
                kj = KJQ * qc + r
                li = cum_L[ci] + kj
                v.wait_ge(s_exp, blk2ei[li])
                sl = slice(cb * PTW + QCW * kj, cb * PTW + QCW * kj + QCW)
                v.tensor_tensor(pT[:, sl], pT[:, sl],
                                masks_sb[:, QCW * r:QCW * r + QCW],
                                ALU.mult).then_inc(s_msk, 1)
            if ci == 0:
                v.wait_ge(s_gs[0], 16 * ((FT + 1) // 2))
                v.wait_ge(s_gs[1], 16 * (FT // 2))
            v.wait_ge(s_att, ci + 1)
            if ci >= 1:
                v.wait_ge(s_ae, 2 * ci)
            v.tensor_copy(rb_at[:, :], ps_drb[:, :]).then_inc(s_dnb, 1)
            if ci >= 2:
                hp, _ = chunks[ci - 2]
                v.wait_ge(s_ci[hp][ci % 2],
                          16 * 2 * DBF * ci_ord[ci - 2])
            v.wait_ge(s_dnb, ci + 1)
            ab = (ci % 2) * 2 * QCW
            for dt in range(2):
                v.tensor_tensor(att_ev[:, ab + QCW * dt:ab + QCW * dt + QCW],
                                psO[dt][:, :], rb_at[:, :],
                                ALU.mult).then_inc(s_ae, 1)

    @blk.sync
    def _(sy):
        sy.wait_ge(s_gm, FT)
        sy.wait_ge(s_spl[0], 16 * DT)
        sy.wait_ge(s_spl[1], 16 * DT)
        for _k in range(4):
            sy.wait_ge(s_vs[_k], 16 * (KJT * NH // 4))
        if debug:
            sy.wait_ge(s_xms, 16 * ET)
        for ft in range(2):
            sy.dma_start(out=xrr[:, (ft % 2) * T:(ft % 2) * T + T],
                         in_=xresT[128 * ft:128 * ft + 128, :]
                         ).then_inc(s_xr[ft % 2], 16)
        emit_qk_load(sy, 0)
        emit_qk_load(sy, 1)
        sy.wait_ge(s_gs[0], 16 * ((FT + 1) // 2))
        sy.wait_ge(s_gs[1], 16 * (FT // 2))
        emit_v_load(sy, 0)
        emit_v_load(sy, 1)
        for ci, (h, qc) in enumerate(chunks):
            sy.wait_ge(s_ae, 2 * (ci + 1))
            ab = (ci % 2) * 2 * QCW
            for dt in range(2):
                sy.dma_start(
                    out=cc_in[h][qc // 2][128 * dt:128 * dt + 128,
                                          QCW * (qc % 2):QCW * (qc % 2) + QCW],
                    in_=att_ev[:, ab + QCW * dt:ab + QCW * dt + QCW]
                ).then_inc(s_ci[h][ci % 2], 16)
            if debug:
                for dt in range(2):
                    sy.dma_start(
                        out=at_tap[HD * h + 128 * dt:HD * h + 128 * dt + 128,
                                   QCW * qc:QCW * qc + QCW],
                        in_=att_ev[:, ab + QCW * dt:ab + QCW * dt + QCW]
                    ).then_inc(s_ci[h][ci % 2], 16)
            if qc == NQC - 1 and h + 2 < NH:
                sy.wait_ge(s_att, NQC * (h + 1))
                emit_head_load(sy, h + 2)

    @blk.gpsimd
    def _(g):
        g.wait_ge(s_gm, FT)
        g.dma_start(out=masks_sb[:, :], in_=masks[:, :]).then_inc(s_pre, 16)
        for ft in range(2):
            g.dma_start(out=wor[:, (ft % 2) * WB:(ft % 2) * WB + WB],
                        in_=wo[ft, :, :]).then_inc(s_wo[ft % 2], 16)
        for h in range(NH):
            for hf in range(2):
                for p in range(2):
                    g.wait_ge(s_ci[h][p], 16 * 2 * DBF * (hf + 1))
                g.collective_compute(
                    "AllGather", ALU.bypass,
                    replica_groups=[[0, 1], [2, 3], [4, 5], [6, 7]],
                    ins=[cc_in[h][hf].ap().opt()],
                    outs=[cc_out[h][hf].ap().opt()],
                ).then_inc(s_cc, 1)
                gi = h // 2
                if gi in EARLY_G:
                    g.wait_ge(s_cc, 2 * h + hf + 1)
                    for r, hh, ro in cc_rows(gi):
                        if hh != h:
                            continue
                        dt = RG * gi + r
                        g.dma_start(
                            out=big2a[:, T * dt + (T // 2) * hf:
                                      T * dt + (T // 2) * hf + T // 2],
                            in_=cc_out[h][hf][ro:ro + 128, :]
                        ).then_inc(s_bl[gi], 16)

    ps3.close()
    p3.close()

    p4 = contextlib.ExitStack()
    big2b = p4.enter_context(nc.sbuf_tensor("big2b", [128, (ET - BG2A) * T], BF16))
    ps4 = contextlib.ExitStack()
    pss = [ps4.enter_context(nc.psum_tensor(f"pss{j}", [128, T], F32))
           for j in range(2)]

    def big2v(dt):
        if dt < BG2A:
            return big2a[:, bsl(dt)]
        return big2b[:, T * (dt - BG2A):T * (dt - BG2A) + T]

    @blk.scalar
    def _(s):
        for hh in range(NH):
            for p in range(2):
                s.wait_ge(s_ci[hh][p], 16 * 2 * DBF * CPP[(hh, p)])
        for gi in range(G):
            if gi in EARLY_G:
                continue
            for h in (2 * gi, 2 * gi + 1):
                for hf in range(2):
                    s.wait_ge(s_cc, 2 * h + hf + 1)
                    for r, hh, ro in cc_rows(gi):
                        if hh != h:
                            continue
                        dt = RG * gi + r
                        b2 = big2v(dt)
                        s.dma_start(out=b2[:, (T // 2) * hf:(T // 2) * hf + T // 2],
                                    in_=cc_out[h][hf][ro:ro + 128, :]
                                    ).then_inc(s_bl[gi], 16)
        for ft in range(2, FT):
            s.wait_ge(s_c1, ft - 1)
            s.dma_start(out=gtr[:, (ft % 2) * T:(ft % 2) * T + T],
                        in_=g_d[128 * ft:128 * ft + 128, :]).then_inc(s_gl[ft % 2], 16)

    @blk.gpsimd
    def _(g):
        for ft in range(2, FT):
            g.wait_ge(s_sm, ft - 1)
            g.dma_start(out=wor[:, (ft % 2) * WB:(ft % 2) * WB + WB],
                        in_=wo[ft, :, :]).then_inc(s_wo[ft % 2], 16)

    @blk.tensor
    def _(t):
        for ft in range(FT):
            sl = ft % 2
            t.wait_ge(s_wo[ft % 2], 16 * (ft // 2 + 1))
            if ft >= 2:
                t.wait_ge(s_c1, ft - 1)
            wof = sl * WB
            for dt in range(ET):
                if ft == 0 and dt % RG == 0:
                    t.wait_ge(s_bl[dt // RG], 16 * 2 * RG)
                b2 = big2v(dt)
                for c in range(NCC):
                    ins = t.matmul(
                        pss[sl][:, QCW * c:QCW * c + QCW],
                        wor[:, wof + 128 * dt:wof + 128 * dt + 128],
                        b2[:, QCW * c:QCW * c + QCW],
                        start=(dt == 0 and c % SPB == 0), stop=(dt == ET - 1),
                        skip_group_check=True)
                    if dt == ET - 1 and c == NCC - 1:
                        ins.then_inc(s_sm, 1)

    @blk.vector
    def _(v):
        for ft in range(FT):
            sl = slice((ft % 2) * T, (ft % 2) * T + T)
            v.wait_ge(s_sm, ft + 1)
            v.wait_ge(s_gl[ft % 2], 16 * (ft // 2 + 1))
            if ft >= 2:
                v.wait_ge(s_c2, ft - 1)
            v.tensor_tensor(two[:, sl], pss[ft % 2][:, :], gtr[:, sl],
                            ALU.mult).then_inc(s_c1, 1)
            v.wait_ge(s_xr[ft % 2], 16 * (ft // 2 + 1))
            v.wait_ge(s_c1, ft + 1)
            v.tensor_tensor(xrr[:, sl], two[:, sl], xrr[:, sl],
                            ALU.add).then_inc(s_c2, 1)

    @blk.sync
    def _(sy):
        for hh in range(NH):
            for p in range(2):
                sy.wait_ge(s_ci[hh][p], 16 * 2 * DBF * CPP[(hh, p)])
        for ft in range(FT):
            if ft >= 2:
                sy.wait_ge(s_c2, ft - 1)
                sy.dma_start(out=outT[128 * (ft - 2):128 * (ft - 2) + 128, :],
                             in_=xrr[:, (ft % 2) * T:(ft % 2) * T + T]
                             ).then_inc(s_st, 16)
                sy.wait_ge(s_st, 16 * (ft - 1))
                sy.dma_start(out=xrr[:, (ft % 2) * T:(ft % 2) * T + T],
                             in_=xresT[128 * ft:128 * ft + 128, :]
                             ).then_inc(s_xr[ft % 2], 16)
        for ft in (FT - 2, FT - 1):
            sy.wait_ge(s_c2, ft + 1)
            sy.dma_start(out=outT[128 * ft:128 * ft + 128, :],
                         in_=xrr[:, (ft % 2) * T:(ft % 2) * T + T]
                         ).then_inc(s_st, 16)
        sy.wait_ge(s_st, 16 * FT)

    ps4.close()
    p4.close()
    p4pre.close()
    ctx.close()
    return nc


def kernel(**inputs):
    cfg = REAL
    maps = make_inputs(inputs, cfg)
    nc = build_program(cfg, debug=False)
    res = run_bass_kernel_spmd(nc, maps, core_ids=list(range(8)))
    out = np.empty((4, cfg.T, cfg.E), dtype=np.float32)
    for c in range(8):
        b, h = c >> 1, c & 1
        out[b][:, cfg.FH * h:cfg.FH * h + cfg.FH] = res.results[c]["outT"].T
    return out
